# version 06
# baseline (speedup 1.0000x reference)
import numpy as np
import ml_dtypes
import concourse.bass as bass
import concourse.mybir as mybir
from concourse.bass_utils import run_bass_kernel_spmd
from concourse.alu_op_type import AluOpType as OP
from contextlib import ExitStack

F32 = mybir.dt.float32
BF16 = mybir.dt.bfloat16
AF = mybir.ActivationFunctionType
BFD = ml_dtypes.bfloat16

DEBUG = 0

B_, S_, D_ = 2, 8192, 1024
NT1 = 16
LQ = 2688
NKC = 21
LO = 2048
HALO = 648
HID = 2730
FCS = [128] * 21 + [42]
FCO = [sum(FCS[:i]) for i in range(22)]
SWAP32 = [i ^ 1 for i in range(32)]
QT = [(o, min(512, LQ - o)) for o in range(0, LQ, 512)]


def build():
    nc = bass.Bass("TRN2", target_bir_lowering=False, debug=False)

    def I(name, shape, dt=F32):
        return nc.dram_tensor(name, list(shape), dt, kind="ExternalInput").ap()

    def Oe(name, shape, dt=F32):
        return nc.dram_tensor(name, list(shape), dt, kind="ExternalOutput").ap()

    x_fm = I("x_fm", [D_, S_])
    x_loc = I("x_loc", [D_, 3072])
    watt = I("watt", [D_, 768])
    woT = I("woT", [256, D_])
    ropc = I("ropc", [128, S_])
    rops = I("rops", [128, S_])
    wqkvf = [I(f"wqkvf{l}", [D_, 1536]) for l in range(2)]
    wof = [I(f"wof{l}", [512, D_]) for l in range(2)]
    w1a = [I(f"w1a{l}", [D_, HID]) for l in range(2)]
    w1g = [I(f"w1g{l}", [D_, HID]) for l in range(2)]
    w2f = [I(f"w2f{l}", [HID, D_]) for l in range(2)]
    qs_col = I("qs_col", [1, 128])
    ks_col = I("ks_col", [1, 128])
    lg_col = I("lg_col", [128, 16])
    lb_col = I("lb_col", [128, 16])
    b1a_col = I("b1a_col", [128, 44])
    b1g_col = I("b1g_col", [128, 44])
    b2_col = I("b2_col", [128, 16])
    xpAq = I("xpAq", [128, LQ + 8])
    xpBq = I("xpBq", [128, LQ + 8])
    xpAk = I("xpAk", [128, LQ + 8])
    xpBk = I("xpBk", [128, LQ + 8])
    tri_lo = I("tri_lo", [128, 128], BF16)
    tri_hi = I("tri_hi", [128, 128], BF16)
    fw01 = I("fw01", [128, 128], BF16)
    sel_cols = I("sel_cols", [128, 4])
    ones128_bf = I("ones128_bf", [128, 1], BF16)
    ones64_bf = I("ones64_bf", [64, 1], BF16)
    ones1x128 = I("ones1x128", [1, 128])
    ones1x64 = I("ones1x64", [1, 64])
    indh2_bf = I("indh2_bf", [128, 2], BF16)
    ident_bf = I("ident_bf", [128, 64], BF16)
    ident_f32 = I("ident_f32", [128, 64])

    y_out = Oe("y", [D_, LO])
    dbg = {}
    if DEBUG:
        dbg["q"] = Oe("dbg_q", [128, 2, S_], BF16)
        dbg["k"] = Oe("dbg_k", [128, 2, S_], BF16)
        dbg["onorm"] = Oe("dbg_onorm", [128, 2, S_], BF16)
        dbg["proj"] = Oe("dbg_proj", [D_, LO])
        dbg["xn"] = Oe("dbg_xn", [128, 8, LQ + 8], BF16)
        dbg["qh"] = Oe("dbg_qh", [128, 4, LQ], BF16)
        dbg["kh"] = Oe("dbg_kh", [128, 4, LQ], BF16)
        dbg["oloc"] = Oe("dbg_oloc", [128, 4, 2176], BF16)
        dbg["x1"] = Oe("dbg_x1", [128, 8, 2049])

    cc_p_in = nc.dram_tensor("cc_p_in", [4 * D_, LO], F32)
    cc_p_out = nc.dram_tensor("cc_p_out", [D_, LO], F32)
    cc_f_in = nc.dram_tensor("cc_f_in", [4 * D_, HALO], BF16)
    cc_f_out = nc.dram_tensor("cc_f_out", [D_, HALO], BF16)
    xn_dram = nc.dram_tensor("xn_dram", [128, 8, 2049], F32)
    xcA = nc.dram_tensor("xcA", [128, 8, 2049], F32)
    xcB = nc.dram_tensor("xcB", [128, 8, 2049], F32)

    outer = ExitStack()
    E = outer.enter_context

    chain = E(nc.semaphore("chain"))
    dsy_h = E(nc.semaphore("dsy"))
    dgp_h = E(nc.semaphore("dgp"))
    ccp = E(nc.semaphore("ccp"))
    SEM = [E(nc.semaphore(f"pp{i}")) for i in range(8)]
    SEMIX = {id(s): i for i, s in enumerate(SEM)}
    scnt = [0] * 8
    _P1PB = globals().get("_PASS1_PINCBY")

    ps_all = E(nc.psum_tensor("ps_all", [128, 3584], F32))

    def PS(a, b):
        return ps_all[:, a:b]

    st = {"chain": 0, "dsy": 0, "dgp": 0, "qn": None}
    hold = {}
    incof = {}

    ENGS = ("tensor", "vector", "scalar", "gpsimd", "sync")
    ASEM = {en: E(nc.semaphore(f"as_{en}")) for en in ENGS}
    acnt = {en: 0 for en in ENGS}
    LS = [E(nc.semaphore(f"ls{i}")) for i in range(8)]
    lcnt = [0] * 8
    LSID = {id(s): ("ls", i) for i, s in enumerate(LS)}
    est = {en: {"w": [], "r": [], "out": None} for en in ENGS}
    eidx = {en: 0 for en in ENGS}
    qclk = {en: {} for en in ENGS}
    insinfo = {}
    pincby = {}
    chain_cum = [{}]
    pend_dma = {"dsy": [], "dgp": []}
    unres = {en: [] for en in ENGS}
    defA = []
    defB = []
    autostats = {"bar": 0, "adw": 0, "warn": 0}

    def _merge(dst, src):
        for k, v in src.items():
            if dst.get(k, -1) < v:
                dst[k] = v

    def _join(a, b):
        d = dict(a)
        _merge(d, b)
        return d

    def _box(ap):
        t = ap.tensor
        pairs = list(ap.ap)
        off = ap.offset
        name = t.name
        if getattr(t, "base_partition", None) is not None and pairs:
            pitch = pairs[0][0]
            lo = hi = 0
            for s0, z0 in pairs[1:]:
                d = (z0 - 1) * s0
                if d < 0:
                    lo += d
                else:
                    hi += d
            if pitch <= 0 or len(pairs) == 0:
                return (name, 0, 1 << 40, 0, 1 << 40)
            p0 = off // pitch
            e0 = off % pitch
            if e0 + lo < 0 or e0 + hi >= pitch:
                return (name, 0, 1 << 40, 0, 1 << 40)
            return (name, p0, p0 + pairs[0][1] - 1, e0 + lo, e0 + hi)
        lo = hi = off
        for s0, z0 in pairs:
            d = (z0 - 1) * s0
            if d < 0:
                lo += d
            else:
                hi += d
        return (name, 0, 0, lo, hi)

    def _olap(a, b):
        return (a[0] == b[0] and a[2] >= b[1] and b[2] >= a[1]
                and a[4] >= b[3] and b[4] >= a[3])

    def _isap(v):
        return hasattr(v, "tensor") and hasattr(v, "ap") \
            and hasattr(v, "offset")

    def _barrier(raw, eng):
        s = est[eng]
        ins = s["out"]
        s["w"].clear()
        s["r"].clear()
        if ins is None:
            return
        ie, idx, cclk, _ = insinfo[id(ins)]
        if qclk[eng].get(ie, -1) >= idx:
            return
        io = incof.get(id(ins))
        if io is None:
            acnt[eng] += 1
            ins.then_inc(ASEM[eng], 1)
            io = (ASEM[eng], acnt[eng], ins)
            incof[id(ins)] = io
        raw.wait_ge(io[0], io[1])
        _merge(qclk[eng], cclk)
        if qclk[eng].get(ie, -1) < idx:
            qclk[eng][ie] = idx
        autostats["bar"] += 1
        if _os.environ.get("KBARLOG"):
            import traceback as _tb
            for fr in reversed(_tb.extract_stack()):
                if "kernel.py" in fr.filename and fr.lineno < 1700 \
                        and "_barrier" not in (fr.name or "") \
                        and fr.name not in ("w", "body", "group", "sec"):
                    autostats.setdefault("barloc", {}).setdefault(
                        (fr.lineno, fr.name), [0])[0] += 1
                    break

    def _warn(n, qn, detail=""):
        import traceback
        fr = traceback.extract_stack()[-3]
        autostats["warn"] += 1
        print(f"AUTOSYNC WARN: {n} overlaps pending {qn} DMA at "
              f"{fr.filename}:{fr.lineno} {detail}")

    class Rec:

        def __init__(self, e, eng):
            object.__setattr__(self, "_e", e)
            object.__setattr__(self, "eng", eng)
            object.__setattr__(self, "last", ("seq", None))

        def __getattr__(self, n):
            raw = object.__getattribute__(self, "_e")
            a = getattr(raw, n)
            if not callable(a):
                return a
            eng = object.__getattribute__(self, "eng")

            def w(*args, **kw):
                if n == "wait_ge":
                    sem, val = args[0], args[1]
                    if sem is chain:
                        _merge(qclk[eng],
                               chain_cum[min(val, len(chain_cum) - 1)])
                    elif sem is dsy_h or sem is dgp_h or id(sem) in LSID:
                        if id(sem) in LSID:
                            ck = LSID[id(sem)]
                        else:
                            ck = "dsy" if sem is dsy_h else "dgp"
                        for qn2 in ("dsy", "dgp"):
                            pend_dma[qn2] = [t for t in pend_dma[qn2]
                                             if t[3] != ck or t[2] > val]
                        _merge(qclk[eng], {ck: val})
                    else:
                        key = (SEMIX.get(id(sem)), val)
                        c = pincby.get(key)
                        if c is None and _P1PB is not None \
                                and key[0] is not None:
                            c = _P1PB.get(key)
                        if c is not None:
                            _merge(qclk[eng], c)
                        else:
                            unres[eng].append(key)
                    r = a(*args, **kw)
                    object.__setattr__(self, "last", ("seq", None))
                    return r
                if n in ("sem_inc", "nop"):
                    r = a(*args, **kw)
                    object.__setattr__(self, "last", ("seq", None))
                    return r
                if n == "collective_compute":
                    r = a(*args, **kw)
                    object.__setattr__(self, "last", ("dma", r))
                    return r
                wb, rb = [], []

                def add(v, dst):
                    if isinstance(v, (list, tuple)):
                        for x in v:
                            add(x, dst)
                    elif _isap(v):
                        dst.append(_box(v))
                is_dma = n in ("dma_start", "indirect_dma_start")
                if is_dma:
                    add(kw.get("out", args[0] if args else None), wb)
                    add(kw.get("in_", args[1] if len(args) > 1 else None), rb)
                else:
                    for i2, v in enumerate(args):
                        add(v, wb if i2 == 0 else rb)
                    for k2, v in kw.items():
                        add(v, wb if k2 in ("out", "outs", "accum_out")
                            else rb)
                s = est[eng]
                myclk = qclk[eng].get(eng, -1)
                s["w"] = [t for t in s["w"] if t[2] > myclk]
                s["r"] = [t for t in s["r"] if t[1] > myclk]
                haz = False
                for b in rb:
                    if any(_olap(b, pb) for pb, _, _ in s["w"]):
                        haz = True
                        break
                if not haz:
                    for b in wb:
                        if any(_olap(b, pb) and not (n == "matmul"
                                                     and pm == "matmul")
                               for pb, pm, _ in s["w"]):
                            haz = True
                            break
                        if any(_olap(b, pb) for pb, _ in s["r"]):
                            haz = True
                            break
                if haz:
                    _barrier(raw, eng)
                if rb or wb:
                    hits = []
                    for qn in ("dsy", "dgp"):
                        if is_dma and qn == st["qn"]:
                            continue
                        for ent in pend_dma[qn]:
                            if any(ent[1] and _olap(b, ent[0]) for b in rb) \
                               or any(_olap(b, ent[0]) for b in wb):
                                hits.append(ent)
                    if hits:
                        import traceback as _tb
                        fr = _tb.extract_stack()[-3 if is_dma else -2]
                        loc = f"{n}@{fr.lineno}"
                        if is_dma:
                            mC = dict(qclk[eng])
                            mun = tuple(unres[eng])
                            for ent in hits:
                                defB.append((mC, mun, eidx[eng], eng,
                                             ent[3], ent[2], ent[4],
                                             ent[5],
                                             f"{loc} pend={ent[0]}"
                                             f"@{ent[2]}"))
                        else:
                            midx = eidx[eng] + 1
                            for ent in hits:
                                defA.append((ent[4], ent[5], eng, midx,
                                             loc))
                r = a(*args, **kw)
                if is_dma:
                    st["_dmaboxes"] = (wb, rb)
                    object.__setattr__(self, "last", ("dma", r))
                elif r is not None and hasattr(r, "then_inc"):
                    eidx[eng] += 1
                    idx = eidx[eng]
                    insinfo[id(r)] = (eng, idx, dict(qclk[eng]), r)
                    s["w"].extend((b, n, idx) for b in wb)
                    s["r"].extend((b, idx) for b in rb)
                    s["out"] = r
                    object.__setattr__(self, "last", ("eng", r))
                elif wb or rb:
                    raise RuntimeError(f"op {n} returned no instruction")
                else:
                    object.__setattr__(self, "last", ("seq", None))
                return r
            return w

    def group(*pairs):
        n0 = st["chain"]
        st["chain"] += len(pairs)
        single = len(pairs) == 1
        for eng, fn in pairs:
            def body(e, fn=fn, n0=n0, eng=eng, single=single):
                if n0:
                    e.wait_ge(chain, n0)
                    _merge(qclk[eng],
                           chain_cum[min(n0, len(chain_cum) - 1)])
                rec = Rec(e, eng)
                fn(rec)
                s = est[eng]
                ins = s["out"]
                kind, lins = rec.last
                if single and kind == "eng" and lins is ins \
                        and ins is not None \
                        and incof.get(id(ins)) is None:
                    ins.then_inc(chain, 1)
                    ie, idx, cclk, _ = insinfo[id(ins)]
                    incof[id(ins)] = (chain, n0 + 1, ins)
                    pub = dict(qclk[eng])
                    _merge(pub, cclk)
                    if pub.get(ie, -1) < idx:
                        pub[ie] = idx
                    chain_cum.append(_join(chain_cum[-1], pub))
                else:
                    _barrier(e, eng)
                    e.sem_inc(chain, 1)
                    chain_cum.append(_join(chain_cum[-1], qclk[eng]))
            getattr(hold["blk"], eng)(body)

    def sec(eng, fn):
        group((eng, fn))

    dmalog = {"dsy": [], "dgp": []}
    globals()["DMALOG"] = dmalog

    def dma(e, qn, out, in_, ls=None):
        import traceback as _tb
        dmalog[qn].append(_tb.extract_stack()[-2].lineno)
        st["qn"] = qn
        ins = e.dma_start(out=out, in_=in_)
        if ls is None:
            ins.then_inc(dsy_h if qn == "dsy" else dgp_h, 16)
            st[qn] += 16
            ck, cv = qn, st[qn]
        else:
            ins.then_inc(LS[ls], 16)
            lcnt[ls] += 16
            ck, cv = ("ls", ls), lcnt[ls]
        wb, rb = st.get("_dmaboxes", ([], []))
        en = e.eng
        C = dict(qclk[en])
        un = tuple(unres[en])
        pend_dma[qn].extend([(b, True, cv, ck, C, un) for b in wb]
                            + [(b, False, cv, ck, C, un) for b in rb])
        return ins

    def dwait(e, qn):
        e.wait_ge(dsy_h if qn == "dsy" else dgp_h, st[qn])

    def dma_nc(e, qn, out, in_):
        with nc.allow_non_contiguous_dma(reason="tiny boundary column"):
            return dma(e, qn, out, in_)

    def pinc(ins, i):
        ins.then_inc(SEM[i], 1)
        scnt[i] += 1
        info = insinfo.get(id(ins))
        if info is not None:
            ie, idx, cclk, _ = info
            c = dict(cclk)
            if c.get(ie, -1) < idx:
                c[ie] = idx
            pincby[(i, scnt[i])] = c
        incof[id(ins)] = (SEM[i], scnt[i], ins)
        return ins

    import os as _os
    TRUNCV = int(_os.environ.get("KTRUNC", "99"))

    def trunc_fin(src_ap, cols, stacks=(), wait_ccp=False):
        def fin(e):
            if wait_ccp:
                e.wait_ge(ccp, 1)
            dma(e, "dgp", y_out[0:128, 0:cols], src_ap)
            dwait(e, "dgp")
        sec("gpsimd", fin)
        for s in stacks:
            s.close()
        outer.close()

    c_ones128 = E(nc.sbuf_tensor("c_ones128", [128, 1], BF16))
    c_ones64 = E(nc.sbuf_tensor("c_ones64", [64, 1], BF16))
    c_o1x128 = E(nc.sbuf_tensor("c_o1x128", [1, 128], F32))
    c_o1x64 = E(nc.sbuf_tensor("c_o1x64", [1, 64], F32))
    c_indh2 = E(nc.sbuf_tensor("c_indh2", [128, 2], BF16))
    c_ident = E(nc.sbuf_tensor("c_ident", [128, 64], BF16))
    c_id32 = E(nc.sbuf_tensor("c_id32", [128, 64], F32))
    c_trilo = E(nc.sbuf_tensor("c_trilo", [128, 128], BF16))
    c_trihi = E(nc.sbuf_tensor("c_trihi", [128, 128], BF16))
    c_fw01 = E(nc.sbuf_tensor("c_fw01", [128, 128], BF16))
    c_sel = E(nc.sbuf_tensor("c_sel", [128, 4], F32))
    c_qs = E(nc.sbuf_tensor("c_qs", [1, 128], F32))
    c_ks = E(nc.sbuf_tensor("c_ks", [1, 128], F32))
    c_lg = E(nc.sbuf_tensor("c_lg", [128, 16], F32))
    c_lb = E(nc.sbuf_tensor("c_lb", [128, 16], F32))
    c_b1a = E(nc.sbuf_tensor("c_b1a", [128, 44], F32))
    c_b1g = E(nc.sbuf_tensor("c_b1g", [128, 44], F32))
    c_b2 = E(nc.sbuf_tensor("c_b2", [128, 16], F32))
    stat1 = E(nc.sbuf_tensor("stat1", [1, 512], F32))
    stat2 = E(nc.sbuf_tensor("stat2", [1, 512], F32))
    stat3 = E(nc.sbuf_tensor("stat3", [1, 512], F32))
    mu_bf = E(nc.sbuf_tensor("mu_bf", [1, 512], BF16))
    negr_bf = E(nc.sbuf_tensor("negr_bf", [1, 768], BF16))
    invb = E(nc.sbuf_tensor("invb", [128, 512], F32))
    invb2 = E(nc.sbuf_tensor("invb2", [128, 512], F32))
    ev32 = E(nc.sbuf_tensor("ev32", [128, 512], F32))
    ev32b = E(nc.sbuf_tensor("ev32b", [128, 512], F32))
    rope1 = E(nc.sbuf_tensor("rope1", [128, 512], F32))
    rope2 = E(nc.sbuf_tensor("rope2", [128, 512], F32))
    sqst = E(nc.sbuf_tensor("sqst", [128, 512], BF16))
    vstage = E(nc.sbuf_tensor("vstage", [128, 512], BF16))
    negc = E(nc.sbuf_tensor("negc", [128, 4], F32))
    mx2 = E(nc.sbuf_tensor("mx2", [2, 36], F32))
    lrow = E(nc.sbuf_tensor("lrow", [1, 512], F32))
    lrec = E(nc.sbuf_tensor("lrec", [1, 512], F32))
    lrec2 = E(nc.sbuf_tensor("lrec2", [1, 512], F32))
    obc = E(nc.sbuf_tensor("obc", [64, 512], F32))
    xn_extra = E(nc.sbuf_tensor("xn_extra", [128, 8, 128], F32))
    mid_x = E(nc.sbuf_tensor("mid_x", [128, 8, 128], F32))
    mkw = E(nc.sbuf_tensor("mkw", [64, 24], F32))
    mk_bf = E(nc.sbuf_tensor("mk_bf", [64, 1], BF16))
    absq = E(nc.sbuf_tensor("absq", [64, 128], BF16))
    ngc1 = E(nc.sbuf_tensor("ngc1", [1, 4], F32))
    ngcall = E(nc.sbuf_tensor("ngcall", [128, 17], F32))
    gel = E(nc.sbuf_tensor("gel", [128, 2, 512], BF16))

    blk = E(nc.Block())
    hold["blk"] = blk

    A = ExitStack()
    EA = A.enter_context
    q_bf = EA(nc.sbuf_tensor("q_bf", [128, 2, S_], BF16))
    k_bf = EA(nc.sbuf_tensor("k_bf", [128, 2, S_], BF16))
    vt = [EA(nc.sbuf_tensor(f"vt{h}", [128, 64, 65], BF16)) for h in range(4)]

    A1 = ExitStack()
    E1 = A1.enter_context
    wq_bf = E1(nc.sbuf_tensor("wq_bf", [128, 8, 768], BF16))
    xbf = E1(nc.sbuf_tensor("xbf", [128, 2, 8, 512], BF16))
    xsq = E1(nc.sbuf_tensor("xsq", [128, 8, 512], BF16))
    rc32 = E1(nc.sbuf_tensor("rc32", [128, 512], F32))
    rs32 = E1(nc.sbuf_tensor("rs32", [128, 512], F32))

    def p0_sync(e):
        for dst, src in [(c_o1x128, ones1x128), (c_o1x64, ones1x64),
                         (c_sel, sel_cols), (c_qs, qs_col), (c_ks, ks_col),
                         (c_lg, lg_col), (c_lb, lb_col), (c_b1a, b1a_col),
                         (c_b1g, b1g_col), (c_b2, b2_col)]:
            dma(e, "dsy", dst[:], src)
        for dst, src in [(c_ones128, ones128_bf), (c_ones64, ones64_bf),
                         (c_indh2, indh2_bf), (c_ident, ident_bf),
                         (c_id32, ident_f32),
                         (c_trilo, tri_lo), (c_trihi, tri_hi),
                         (c_fw01, fw01)]:
            dma(e, "dsy", dst[:], src)
        dwait(e, "dsy")
    sec("sync", p0_sync)

    def p0_gp(e):
        dma(e, "dgp", wq_bf[:], watt.rearrange("(c p) f -> p c f", p=128))
        dwait(e, "dgp")
    sec("gpsimd", p0_gp)

    def p0_mm(e):
        for ec in range(8):
            for a, w in [(0, 512), (512, 256)]:
                e.matmul(PS(0, 1024)[0:1, a:a + w], c_ones128[:],
                         wq_bf[:, ec, a:a + w], start=(ec == 0), stop=(ec == 7))
    sec("tensor", p0_mm)

    def p0_v(e):
        e.tensor_scalar(negr_bf[:], PS(0, 1024)[0:1, 0:768], -1.0, None,
                        op0=OP.mult)
        for h in range(4):
            e.memset(vt[h][:, :, 64:65], 1.0)
    sec("vector", p0_v)

    if TRUNCV == 10:
        trunc_fin(wq_bf[:, 0, 0:768], 768, stacks=(A1, A))
        return nc

    KP1 = int(_os.environ.get("KP1", "9"))
    KT1N = int(_os.environ.get("KT1N", str(NT1)))

    for tt in range(KT1N):
        t0 = 512 * tt
        bu = tt % 2

        def p1_ld(e, t0=t0, bu=bu):
            dma(e, "dgp", xbf[:, bu],
                x_fm[:, t0:t0 + 512].rearrange("(c p) t -> p c t", p=128))
            dwait(e, "dgp")
        sec("gpsimd", p1_ld)

        def p1_rt(e, t0=t0):
            dma(e, "dsy", rc32[:], ropc[:, t0:t0 + 512])
            dma(e, "dsy", rs32[:], rops[:, t0:t0 + 512])
            dwait(e, "dsy")
        sec("sync", p1_rt)

        if KP1 < 2:
            continue

        def p1_sq(e, bu=bu):
            return e.activation(xsq[:].rearrange("p c t -> p (c t)"),
                                xbf[:, bu].rearrange("p c t -> p (c t)"),
                                AF.Square, bias=0.0, scale=1.0)
        sec("scalar", p1_sq)

        def p1_sums(e, bu=bu):
            for ec in range(8):
                e.matmul(PS(1024, 1536)[0:1, 0:512], c_ones128[:],
                         xbf[:, bu, ec], start=(ec == 0), stop=(ec == 7))
            for ec in range(8):
                e.matmul(PS(1536, 2048)[0:1, 0:512], c_ones128[:],
                         xsq[:, ec], start=(ec == 0), stop=(ec == 7))
        sec("tensor", p1_sums)

        def p1_st1(e):
            e.tensor_scalar(stat1[:], PS(1024, 1536)[0:1, 0:512], 1.0 / D_,
                            None, op0=OP.mult)
            e.tensor_copy(mu_bf[:], stat1[:])
            e.tensor_tensor(stat2[:], stat1[:], stat1[:], OP.mult)
            e.tensor_scalar(stat2[:], stat2[:], -1e-5, None, op0=OP.add)
            e.scalar_tensor_tensor(stat3[:], PS(1536, 2048)[0:1, 0:512],
                                   1.0 / D_, stat2[:], op0=OP.mult,
                                   op1=OP.subtract)
        sec("vector", p1_st1)
        sec("scalar", lambda e: e.activation(stat3[:], stat3[:], AF.Sqrt,
                                             bias=0.0, scale=1.0))
        sec("vector", lambda e: e.reciprocal(stat1[:], stat3[:]))
        sec("tensor", lambda e: e.matmul(PS(2048, 2560)[:, 0:512], c_o1x128[:],
                                         stat1[:], start=True, stop=True))
        sec("vector", lambda e: e.tensor_copy(invb[:], PS(2048, 2560)[:, 0:512]))

        if KP1 < 3:
            continue

        for fc in range(6):
            a = 512 * (fc % 2)

            def p1_qkv(e, fc=fc, a=a, bu=bu):
                for ec in range(8):
                    e.matmul(PS(0, 1024)[:, a:a + 512],
                             wq_bf[:, ec, 128 * fc:128 * fc + 128],
                             xbf[:, bu, ec], start=(ec == 0), stop=False)
                e.matmul(PS(0, 1024)[:, a:a + 512],
                         negr_bf[0:1, 128 * fc:128 * fc + 128],
                         mu_bf[:], start=False, stop=True)
            sec("tensor", p1_qkv)

            if fc < 4:
                dst = q_bf if fc < 2 else k_bf
                ch = fc % 2

                def p1_rope(e, a=a, dst=dst, ch=ch, t0=t0):
                    e.tensor_tensor(ev32[:], PS(0, 1024)[:, a:a + 512],
                                    invb[:], OP.mult)
                    e.stream_shuffle(rope1[:], ev32[:], mask=SWAP32)
                    e.tensor_tensor(rope2[:], ev32[:], rc32[:], OP.mult)
                    e.tensor_tensor(rope1[:], rope1[:], rs32[:], OP.mult)
                    e.tensor_tensor(dst[:, ch, t0:t0 + 512],
                                    rope2[:], rope1[:], OP.add)
                if KP1 >= 4:
                    sec("vector", p1_rope)
            else:
                ch = fc - 4

                def p1_evv(e, a=a):
                    e.tensor_tensor(vstage[:], PS(0, 1024)[:, a:a + 512],
                                    invb[:], OP.mult)
                if KP1 < 5:
                    continue
                sec("vector", p1_evv)
                if KP1 < 6:
                    continue

                def p1_vtr(e):
                    for hb in (0, 1):
                        for j in range(4):
                            b0 = 2048 if hb == 0 else 3072
                            e.matmul(ps_all[:, b0 + 64 * j:b0 + 64 * j + 64],
                                     vstage[64 * hb:64 * hb + 64,
                                            128 * j:128 * j + 128],
                                     c_ident[64 * hb:64 * hb + 64, :],
                                     start=True, stop=True)
                sec("tensor", p1_vtr)
                if KP1 < 7:
                    continue

                def p1_vcp(e, ch=ch, tt=tt):
                    for hb in (0, 1):
                        for j in range(4):
                            b0 = 2048 if hb == 0 else 3072
                            e.tensor_copy(vt[2 * ch + hb][:, 4 * tt + j, 0:64],
                                          ps_all[:, b0 + 64 * j:
                                                 b0 + 64 * j + 64])
                sec("vector", p1_vcp)

    if DEBUG:
        def pd1(e):
            dma(e, "dsy", dbg["q"], q_bf[:])
            dma(e, "dsy", dbg["k"], k_bf[:])
            dwait(e, "dsy")
        sec("sync", pd1)

    if TRUNCV == 1:
        trunc_fin(q_bf[:, 0, 0:2048], 2048, stacks=(A1, A))
        return nc

    A1.close()
    A2 = ExitStack()
    EA2 = A2.enter_context
    o_norm = EA2(nc.sbuf_tensor("o_norm", [128, 2, S_], BF16))
    woT_bf = EA2(nc.sbuf_tensor("woT_bf", [128, 2, 1024], BF16))
    pt_sb = EA2(nc.sbuf_tensor("pt_sb", [128, 2, 1024], BF16))
    evp = EA2(nc.sbuf_tensor("evp", [128, 8, 512], F32))

    def pA2_gp(e):
        dma(e, "dgp", woT_bf[:], woT.rearrange("(c p) f -> p c f", p=128))
        dwait(e, "dgp")
    sec("gpsimd", pA2_gp)

    for hp in range(2):
        for mrow, buf in ((0, q_bf), (1, k_bf)):
            bA = scnt[0]
            bB = scnt[1]

            def pp_sq(e, buf=buf, hp=hp, bB=bB):
                for t in range(NT1):
                    if t >= 2:
                        e.wait_ge(SEM[1], bB + t - 1)
                    par = sqst if t % 2 == 0 else vstage
                    pinc(e.activation(par[:], buf[:, hp, 512 * t:512 * t + 512],
                                      AF.Square, bias=0.0, scale=1.0), 0)

            def pp_mm(e, bA=bA, bC2=scnt[2]):
                for t in range(NT1):
                    e.wait_ge(SEM[0], bA + t + 1)
                    if t >= 2:
                        e.wait_ge(SEM[2], bC2 + t - 1)
                    par = sqst if t % 2 == 0 else vstage
                    po = 2560 + 512 * (t % 2)
                    pinc(e.matmul(ps_all[0:2, po:po + 512], c_indh2[:], par[:],
                                  start=True, stop=True), 1)

            def pp_rd(e, bB=bB, mrow=mrow):
                for t in range(NT1):
                    e.wait_ge(SEM[1], bB + t + 1)
                    po = 2560 + 512 * (t % 2)
                    pinc(e.tensor_reduce(mx2[0:2, 4 + t:5 + t],
                                         ps_all[0:2, po:po + 512],
                                         axis=mybir.AxisListType.X, op=OP.max),
                         2)
                e.tensor_reduce(mx2[0:2, mrow:mrow + 1], mx2[0:2, 4:20],
                                axis=mybir.AxisListType.X, op=OP.max)
            group(("scalar", pp_sq), ("tensor", pp_mm), ("vector", pp_rd))

        sec("vector", lambda e: e.tensor_tensor(mx2[0:2, 2:3], mx2[0:2, 0:1],
                                                mx2[0:2, 1:2], OP.mult))
        sec("scalar", lambda e: e.activation(mx2[0:2, 2:3], mx2[0:2, 2:3],
                                             AF.Sqrt, bias=0.0, scale=1.0))
        sec("gpsimd", lambda e: e.tensor_reduce(mx2[0:1, 3:4], mx2[0:2, 2:3],
                                                axis=mybir.AxisListType.C,
                                                op=OP.max))
        sec("vector", lambda e: e.tensor_scalar(mx2[0:1, 3:4], mx2[0:1, 3:4],
                                                -0.125, None, op0=OP.mult))
        sec("tensor", lambda e: e.matmul(PS(2048, 2560)[:, 0:1], c_o1x128[:],
                                         mx2[0:1, 3:4], start=True, stop=True))
        sec("vector", lambda e, hp=hp: e.tensor_copy(negc[:, hp:hp + 1],
                                                     PS(2048, 2560)[:, 0:1]))

    if TRUNCV == 2:
        trunc_fin(q_bf[:, 0, 0:2048], 2048, stacks=(A2, A))
        return nc

    for hp in range(2):
        for qs in range(NT1):
            nkc = 4 * qs + 4
            bA = scnt[0]; bB = scnt[1]; bC = scnt[2]; bD = scnt[3]
            bE = scnt[4]; bF = scnt[5]; bG = scnt[6]

            def p2_t(e, hp=hp, qs=qs, nkc=nkc, bB=bB, bC=bC, bE=bE, bG=bG):
                for v in range(nkc + 1):
                    if v < nkc:
                        kc = v
                        dg = kc - 4 * qs
                        off = 128 * dg if dg >= 0 else 0
                        if kc >= 2:
                            e.wait_ge(SEM[1], bB + kc - 1)
                        sb0 = 1024 * (kc % 2)
                        for h in (0, 1):
                            ins = e.matmul(
                                ps_all[:, sb0 + 512 * h + off:
                                       sb0 + 512 * h + 512],
                                k_bf[64 * h:64 * h + 64, hp,
                                     128 * kc:128 * kc + 128],
                                q_bf[64 * h:64 * h + 64, hp,
                                     512 * qs + off:512 * qs + 512],
                                start=True, stop=True,
                                tile_position=(64 * h, 0))
                        pinc(ins, 0)
                    if v >= 1:
                        kc = v - 1
                        dg = kc - 4 * qs
                        off = 128 * dg if dg >= 0 else 0
                        if dg >= 0:
                            e.wait_ge(SEM[2], bC + dg + 1)
                        else:
                            e.wait_ge(SEM[1], bB + kc + 1)
                        for h in (0, 1):
                            ob = 2048 + 512 * h
                            ins = e.matmul(
                                ps_all[0:65, ob + off:ob + 512],
                                vt[2 * hp + h][:, kc, :],
                                pt_sb[:, kc % 2, 512 * h + off:512 * h + 512],
                                start=(kc == 0), stop=(kc == nkc - 1))
                        pinc(ins, 3)
                for h in (0, 1):
                    e.wait_ge(SEM[4], bE + h + 1)
                    if h == 1:
                        e.wait_ge(SEM[6], bG + 1)
                    src = lrec if h == 0 else lrec2
                    pinc(e.matmul(PS(3072, 3584)[0:64, 0:512], c_o1x64[:],
                                  src[0:1, 0:512], start=True, stop=True), 5)

            def p2_s(e, hp=hp, qs=qs, nkc=nkc, bA=bA, bD=bD):
                for kc in range(nkc):
                    dg = kc - 4 * qs
                    off = 128 * dg if dg >= 0 else 0
                    e.wait_ge(SEM[0], bA + kc + 1)
                    if kc >= 2:
                        e.wait_ge(SEM[3], bD + kc - 1)
                    sb0 = 1024 * (kc % 2)
                    src = ps_all[:, sb0:sb0 + 1024].rearrange(
                        "p (h q) -> p h q", h=2)[:, :, off:512]
                    dstv = pt_sb[:, kc % 2].rearrange(
                        "p (h q) -> p h q", h=2)[:, :, off:512]
                    pinc(e.activation(dstv, src, AF.Exp,
                                      bias=negc[:, hp:hp + 1], scale=0.125), 1)

            def p2_m(e, qs=qs, nkc=nkc, bB=bB):
                for kc in range(4 * qs, nkc):
                    dg = kc - 4 * qs
                    e.wait_ge(SEM[1], bB + kc + 1)
                    off = 128 * dg
                    for h in (0, 1):
                        ins = e.tensor_tensor(
                            pt_sb[:, kc % 2,
                                  512 * h + off:512 * h + off + 128],
                            pt_sb[:, kc % 2,
                                  512 * h + off:512 * h + off + 128],
                            c_trilo[:], OP.mult)
                    pinc(ins, 2)

            def p2_v(e, hp=hp, qs=qs, nkc=nkc, bD=bD, bF=bF):
                e.wait_ge(SEM[3], bD + nkc)
                for h in (0, 1):
                    ob = 2048 + 512 * h
                    dstr = lrec if h == 0 else lrec2
                    e.tensor_scalar(lrow[:], ps_all[64:65, ob:ob + 512], 1e-30,
                                    None, op0=OP.add)
                    pinc(e.reciprocal(dstr[:], lrow[:]), 4)
                for h in (0, 1):
                    e.wait_ge(SEM[5], bF + h + 1)
                    e.tensor_copy(obc[:], PS(3072, 3584)[0:64, 0:512])
                    pinc(e.tensor_tensor(
                        o_norm[64 * h:64 * h + 64, hp,
                               512 * qs:512 * qs + 512],
                        ps_all[0:64, 2048 + 512 * h:2560 + 512 * h],
                        obc[:], OP.mult), 6)
            group(("tensor", p2_t), ("scalar", p2_s), ("gpsimd", p2_m),
                  ("vector", p2_v))

    if DEBUG:
        def pd2(e):
            dma(e, "dsy", dbg["onorm"], o_norm[:])
            dwait(e, "dsy")
        sec("sync", pd2)

    if TRUNCV == 3:
        trunc_fin(o_norm[:, 0, 0:2048], 2048, stacks=(A2, A))
        return nc

    bA = scnt[0]
    bH = scnt[7]
    bL6 = lcnt[6]

    def p3_t(e, bH=bH):
        for g2 in range(8 * NT1):
            tt, fc = divmod(g2, 8)
            a = 512 * (fc % 4)
            if g2 >= 4:
                e.wait_ge(SEM[7], bH + g2 - 3)
            for ec in range(2):
                ins = e.matmul(PS(0, 2048)[:, a:a + 512],
                               woT_bf[:, ec, 128 * fc:128 * fc + 128],
                               o_norm[:, ec, 512 * tt:512 * tt + 512],
                               start=(ec == 0), stop=(ec == 1))
            pinc(ins, 0)

    def p3_v(e, bA=bA, bL6=bL6):
        for g2 in range(8 * NT1):
            tt, fc = divmod(g2, 8)
            a = 512 * (fc % 4)
            e.wait_ge(SEM[0], bA + g2 + 1)
            if tt >= 1 and fc == 0:
                e.wait_ge(LS[6], bL6 + 16 * tt)
            pinc(e.tensor_copy(evp[:, fc, :], PS(0, 2048)[:, a:a + 512]), 7)

    def p3_st(e, bH=bH, bL6=bL6):
        for tt in range(NT1):
            r = tt // 4
            q4 = tt % 4
            e.wait_ge(SEM[7], bH + 8 * (tt + 1))
            dma(e, "dsy",
                cc_p_in[1024 * r:1024 * r + 1024,
                        512 * q4:512 * q4 + 512].rearrange(
                            "(c p) t -> p c t", p=128),
                evp[:], ls=6)
        e.wait_ge(LS[6], bL6 + 16 * NT1)
    group(("tensor", p3_t), ("vector", p3_v), ("sync", p3_st))

    def p3_cc(e):
        e.collective_compute(
            "ReduceScatter", OP.add,
            replica_groups=[[0, 1, 2, 3], [4, 5, 6, 7]],
            ins=[cc_p_in.ap().opt()], outs=[cc_p_out.ap().opt()],
        ).then_inc(ccp)
    sec("gpsimd", p3_cc)

    if TRUNCV == 4:
        trunc_fin(o_norm[:, 0, 0:2048], 2048, stacks=(A2, A),
                  wait_ccp=True)
        return nc

    A2.close()
    A.close()

    B = ExitStack()
    EB = B.enter_context
    big_bf = EB(nc.sbuf_tensor("big_bf", [128, 22528], BF16))
    xqin = big_bf[:, 0:8 * (LQ + 8)].rearrange("p (c t) -> p c t", c=8)
    o_loc = EB(nc.sbuf_tensor("o_loc", [128, 4, 2176], BF16))
    ptl = EB(nc.sbuf_tensor("ptl", [128, 2, 640], BF16))

    B1 = ExitStack()
    xlt = B1.enter_context(nc.sbuf_tensor("xlt2", [128, 2, 8, 512], F32))
    xcb = B1.enter_context(nc.sbuf_tensor("xcb2", [128, 2, 8, 512], BF16))
    xsqb = B1.enter_context(nc.sbuf_tensor("xsqb", [128, 8, 512], BF16))
    xnst = B1.enter_context(nc.sbuf_tensor("xnst", [128, 8, 512], F32))

    for t in range(6):
        bu = t % 2

        def b1_ld(e, t=t, bu=bu):
            dma(e, "dsy", xlt[:, bu],
                x_loc[:, 512 * t:512 * t + 512].rearrange(
                    "(c p) t -> p c t", p=128))
            dwait(e, "dsy")
        sec("sync", b1_ld)

        def b1_cast(e, t=t, bu=bu):
            dma(e, "dgp", xcb[:, bu],
                x_loc[:, 512 * t:512 * t + 512].rearrange(
                    "(c p) t -> p c t", p=128))
            dwait(e, "dgp")
        sec("gpsimd", b1_cast)

        def b1_sq(e, bu=bu):
            return e.activation(xsqb[:].rearrange("p c t -> p (c t)"),
                                xcb[:, bu].rearrange("p c t -> p (c t)"),
                                AF.Square, bias=0.0, scale=1.0)
        sec("scalar", b1_sq)

        def b1_mm(e, bu=bu):
            for ec in range(8):
                e.matmul(PS(1024, 1536)[0:1, 0:512], c_ones128[:],
                         xcb[:, bu, ec], start=(ec == 0), stop=(ec == 7))
            for ec in range(8):
                e.matmul(PS(1536, 2048)[0:1, 0:512], c_ones128[:],
                         xsqb[:, ec], start=(ec == 0), stop=(ec == 7))
        sec("tensor", b1_mm)

        def b1_st(e):
            e.tensor_scalar(stat1[:], PS(1024, 1536)[0:1, 0:512], 1.0 / D_,
                            None, op0=OP.mult)
            e.tensor_tensor(stat2[:], stat1[:], stat1[:], OP.mult)
            e.tensor_scalar(stat2[:], stat2[:], -1e-5, None, op0=OP.add)
            e.scalar_tensor_tensor(stat3[:], PS(1536, 2048)[0:1, 0:512],
                                   1.0 / D_, stat2[:], op0=OP.mult,
                                   op1=OP.subtract)
        sec("vector", b1_st)
        sec("scalar", lambda e: e.activation(stat3[:], stat3[:], AF.Sqrt,
                                             bias=0.0, scale=1.0))

        def b1_inv(e):
            e.reciprocal(stat2[:], stat3[:])
            e.tensor_tensor(stat1[:], stat1[:], stat2[:], OP.mult)
            e.tensor_scalar(stat1[:], stat1[:], -1.0, None, op0=OP.mult)
        sec("vector", b1_inv)

        def b1_bc(e):
            e.matmul(PS(2048, 2560)[:, 0:512], c_o1x128[:], stat2[:],
                     start=True, stop=True)
            e.matmul(PS(2560, 3072)[:, 0:512], c_o1x128[:], stat1[:],
                     start=True, stop=True)
        sec("tensor", b1_bc)

        def b1_bcc(e):
            e.tensor_copy(invb[:], PS(2048, 2560)[:, 0:512])
            e.tensor_copy(invb2[:], PS(2560, 3072)[:, 0:512])
        sec("vector", b1_bcc)

        def b1_xn(e, t=t, bu=bu):
            c0 = 512 * t - 376
            for ec in range(8):
                e.tensor_tensor(xnst[:, ec, :], xlt[:, bu, ec], invb[:],
                                OP.mult)
                e.tensor_tensor(xnst[:, ec, :], xnst[:, ec, :], invb2[:],
                                OP.add)
                lo = max(0, -c0)
                if c0 + 512 > 0:
                    e.tensor_copy(xqin[:, ec, c0 + lo:c0 + 512],
                                  xnst[:, ec, lo:512])
                if t == 1:
                    e.tensor_copy(xn_extra[:, ec, :], xnst[:, ec, 384:512])
        sec("vector", b1_xn)

        def b1_dma(e, t=t):
            d0 = 512 * t - 1023
            lo = max(0, -d0)
            if d0 + 512 > 0:
                dofn = dma_nc if 512 - lo == 1 else dma
                dofn(e, "dsy", xn_dram[:, :, d0 + lo:d0 + 512],
                     xnst[:, :, lo:512])
                dwait(e, "dsy")
        sec("sync", b1_dma)

    B1.close()
    if TRUNCV == 5:
        trunc_fin(big_bf[:, 0:2048], 2048, stacks=(B,))
        return nc

    if DEBUG:
        def pdxn(e):
            dma(e, "dsy", dbg["xn"], xqin)
            dwait(e, "dsy")
        sec("sync", pdxn)

    for l in range(2):
        LA = ExitStack()
        EL = LA.enter_context
        wqkvf_bf = EL(nc.sbuf_tensor(f"wqkvf_bf{l}", [128, 8, 1536], BF16))
        qfc = EL(nc.sbuf_tensor(f"qfc{l}", [128, 4, LQ], BF16))
        kfc = EL(nc.sbuf_tensor(f"kfc{l}", [128, 4, LQ], BF16))
        vtl = [EL(nc.sbuf_tensor(f"vtl{l}_{h}", [128, NKC, 65], BF16))
               for h in range(8)]
        xpt = EL(nc.sbuf_tensor(f"xpt{l}", [128, 4, 512], F32))

        def l_wload(e, l=l):
            dma(e, "dgp", wqkvf_bf[:],
                wqkvf[l].rearrange("(c p) f -> p c f", p=128))
            dwait(e, "dgp")
        sec("gpsimd", l_wload)

        def l_vones(e):
            for h in range(8):
                e.memset(vtl[h][:, :, 64:65], 1.0)
        sec("vector", l_vones)

        for (qo, qw) in QT:
            def l_tbl(e, qo=qo, qw=qw):
                for i, tbl in enumerate((xpAq, xpBq, xpAk, xpBk)):
                    dma(e, "dsy", xpt[:, i, 0:qw], tbl[:, 8 + qo:8 + qo + qw])
                dwait(e, "dsy")
            sec("sync", l_tbl)

            for fc in range(12):
                a = 512 * (fc % 2)

                if fc < 8:
                    isq = fc < 4
                    ch = fc % 4
                    dstf = qfc if isq else kfc
                    tA, tB = (0, 1) if isq else (2, 3)
                    bT = scnt[0]; bS = scnt[1]; bV = scnt[2]

                    def lq_t(e, fc=fc, a=a, qo=qo, qw=qw, bS=bS, bV=bV,
                             l=l, isq=isq):
                        for ec in range(8):
                            co = 8 + qo if ec < 4 else 7 + qo
                            ins = e.matmul(PS(0, 1024)[:, a:a + qw],
                                           wqkvf_bf[:, ec,
                                                    128 * fc:128 * fc + 128],
                                           xqin[:, ec, co:co + qw],
                                           start=(ec == 0), stop=(ec == 7))
                        pinc(ins, 0)
                        e.wait_ge(SEM[1], bS + 1)
                        e.matmul(ps_all[0:1, 2048:2048 + qw],
                                 c_indh2[:, 0:1], sqst[:, 0:qw],
                                 start=True, stop=True)
                        pinc(e.matmul(ps_all[0:1, 3072:3072 + qw],
                                      c_indh2[:, 1:2], sqst[:, 0:qw],
                                      start=True, stop=True), 0)
                        e.wait_ge(SEM[2], bV + 2)
                        scr = (c_qs if isq else c_ks)[0:1, 64 * l:64 * l + 64]
                        e.matmul(PS(2560, 3072)[0:64, 0:qw], scr,
                                 stat1[0:1, 0:qw], start=True, stop=True)
                        pinc(e.matmul(ps_all[64:128, 2560:2560 + qw],
                                      scr, stat2[0:1, 0:qw],
                                      start=True, stop=True), 0)

                    def lq_s(e, a=a, qw=qw, bT=bT, bV=bV):
                        e.wait_ge(SEM[0], bT + 1)
                        pinc(e.activation(sqst[:, 0:qw],
                                          PS(0, 1024)[:, a:a + qw],
                                          AF.Square, bias=0.0, scale=1.0), 1)
                        e.wait_ge(SEM[2], bV + 1)
                        e.activation(stat1[0:1, 0:qw], stat1[0:1, 0:qw],
                                     AF.Sqrt, bias=0.0, scale=1.0)
                        pinc(e.activation(stat2[0:1, 0:qw], stat2[0:1, 0:qw],
                                          AF.Sqrt, bias=0.0, scale=1.0), 1)

                    def lq_v(e, fc=fc, a=a, qo=qo, qw=qw, ch=ch, dstf=dstf,
                             tA=tA, tB=tB, bT=bT, bS=bS):
                        e.wait_ge(SEM[0], bT + 2)
                        e.tensor_scalar(stat1[0:1, 0:qw],
                                        ps_all[0:1, 2048:2048 + qw],
                                        1e-24, None, op0=OP.max)
                        pinc(e.tensor_scalar(stat2[0:1, 0:qw],
                                             ps_all[0:1, 3072:3072 + qw],
                                             1e-24, None, op0=OP.max), 2)
                        e.wait_ge(SEM[1], bS + 2)
                        e.reciprocal(stat1[0:1, 0:qw], stat1[0:1, 0:qw])
                        pinc(e.reciprocal(stat2[0:1, 0:qw], stat2[0:1, 0:qw]),
                             2)
                        e.wait_ge(SEM[0], bT + 3)
                        e.tensor_copy(invb[:, 0:qw],
                                      ps_all[0:128, 2560:2560 + qw])
                        e.tensor_tensor(ev32[:, 0:qw],
                                        PS(0, 1024)[:, a:a + qw],
                                        invb[:, 0:qw], OP.mult)
                        e.stream_shuffle(rope1[:, 0:qw], ev32[:, 0:qw],
                                         mask=SWAP32)
                        e.tensor_tensor(rope2[:, 0:qw], ev32[:, 0:qw],
                                        xpt[:, tA, 0:qw], OP.mult)
                        e.tensor_tensor(rope1[:, 0:qw], rope1[:, 0:qw],
                                        xpt[:, tB, 0:qw], OP.mult)
                        e.tensor_tensor(dstf[:, ch, qo:qo + qw],
                                        rope2[:, 0:qw], rope1[:, 0:qw],
                                        OP.add)
                    group(("tensor", lq_t), ("scalar", lq_s), ("vector", lq_v))
                else:
                    ch = fc - 8

                    def lv_t(e, fc=fc, a=a, qo=qo, qw=qw):
                        for ec in range(8):
                            co = 8 + qo if ec < 4 else 7 + qo
                            e.matmul(PS(0, 1024)[:, a:a + qw],
                                     wqkvf_bf[:, ec, 128 * fc:128 * fc + 128],
                                     xqin[:, ec, co:co + qw],
                                     start=(ec == 0), stop=(ec == 7))
                    sec("tensor", lv_t)

                    def lv_ev(e, a=a, qw=qw):
                        e.tensor_copy(vstage[:, 0:qw], PS(0, 1024)[:, a:a + qw])
                    sec("vector", lv_ev)

                    def lv_tr(e, qw=qw):
                        for hb in (0, 1):
                            for j in range(qw // 128):
                                sl = 64 * (2 * j + hb)
                                b0 = 2048 if hb == 0 else 3072
                                e.matmul(ps_all[:, b0 + 64 * j:
                                                b0 + 64 * j + 64],
                                         vstage[64 * hb:64 * hb + 64,
                                                128 * j:128 * j + 128],
                                         c_ident[64 * hb:64 * hb + 64, :],
                                         start=True, stop=True)
                    sec("tensor", lv_tr)

                    def lv_cp(e, ch=ch, qo=qo, qw=qw):
                        for hb in (0, 1):
                            for j in range(qw // 128):
                                b0 = 2048 if hb == 0 else 3072
                                e.tensor_copy(
                                    vtl[2 * ch + hb][:, qo // 128 + j, 0:64],
                                    ps_all[:, b0 + 64 * j:b0 + 64 * j + 64])
                    sec("vector", lv_cp)

        if DEBUG and l == 0:
            def pdqk(e):
                dma(e, "dsy", dbg["qh"], qfc[:])
                dma(e, "dsy", dbg["kh"], kfc[:])
                dwait(e, "dsy")
            sec("sync", pdqk)

        for h in range(8):
            ch, hb = h // 2, h % 2

            if hb == 0:
                bS0 = scnt[0]; bS1 = scnt[1]

                def nb_s(e, ch=ch, bS1=bS1):
                    for ci in range(38):
                        if ci >= 2:
                            e.wait_ge(SEM[1], bS1 + ci - 1)
                        par = sqst if ci % 2 == 0 else vstage
                        if ci < 17:
                            src = qfc[:, ch, 512 + 128 * ci:640 + 128 * ci]
                        else:
                            kc = ci - 17
                            src = kfc[:, ch, 128 * kc:128 * kc + 128]
                        pinc(e.activation(par[:, 0:128], src, AF.Abs,
                                          bias=0.0, scale=1.0), 0)

                def nb_v(e, bS0=bS0):
                    for ci in range(38):
                        e.wait_ge(SEM[0], bS0 + ci + 1)
                        par = sqst if ci % 2 == 0 else vstage
                        pinc(e.tensor_reduce(invb[:, ci:ci + 1], par[:, 0:128],
                                             axis=mybir.AxisListType.X,
                                             op=OP.max), 1)
                    for i in range(17):
                        e.tensor_reduce(invb[:, 40 + i:41 + i],
                                        invb[:, 17 + i:22 + i],
                                        axis=mybir.AxisListType.X, op=OP.max)
                    e.tensor_tensor(ev32[:, 0:17], invb[:, 0:17],
                                    invb[:, 40:57], OP.mult)
                    e.tensor_copy(sqst[:, 0:17], ev32[:, 0:17])
                group(("scalar", nb_s), ("vector", nb_v))

                def nb_t(e):
                    e.matmul(ps_all[0:1, 2048:2065], c_indh2[:, 0:1],
                             sqst[:, 0:17], start=True, stop=True)
                    e.matmul(ps_all[0:1, 3072:3089], c_indh2[:, 1:2],
                             sqst[:, 0:17], start=True, stop=True)
                sec("tensor", nb_t)

                def nb_v2(e):
                    e.tensor_scalar(stat1[0:1, 0:17], ps_all[0:1, 2048:2065],
                                    -8.24, 24.0, op0=OP.mult, op1=OP.add)
                    e.tensor_scalar(stat2[0:1, 0:17], ps_all[0:1, 3072:3089],
                                    -8.24, 24.0, op0=OP.mult, op1=OP.add)
                sec("vector", nb_v2)

                def nb_t2(e):
                    e.matmul(PS(2560, 3072)[:, 0:17], c_o1x128[:],
                             stat1[0:1, 0:17], start=True, stop=True)
                    e.matmul(ps_all[:, 3328:3345], c_o1x128[:],
                             stat2[0:1, 0:17], start=True, stop=True)
                sec("tensor", nb_t2)

                def nb_v3(e):
                    e.tensor_copy(ngcall[:, 0:17], PS(2560, 3072)[:, 0:17])
                    e.tensor_copy(invb[:, 100:117], ps_all[:, 3328:3345])
                sec("vector", nb_v3)

            bA = scnt[0]; bB = scnt[1]; bC = scnt[2]; bD = scnt[3]
            bE = scnt[4]; bF = scnt[5]
            bG6 = scnt[6]

            def la_t(e, h=h, ch=ch, hb=hb, bB=bB, bC=bC, bE=bE, bF=bF):
                for v in range(19):
                    if v < 17:
                        i = v
                        qc = 512 + 128 * i
                        if i >= 2:
                            e.wait_ge(SEM[1], bB + i - 1)
                        sb0 = 1024 * (i % 2)
                        for k5 in range(5):
                            ins = e.matmul(
                                ps_all[:, sb0 + 128 * k5:sb0 + 128 * k5 + 128],
                                kfc[64 * hb:64 * hb + 64, ch,
                                    128 * (i + k5):128 * (i + k5) + 128],
                                qfc[64 * hb:64 * hb + 64, ch, qc:qc + 128],
                                start=True, stop=True,
                                tile_position=(64 * hb, 0))
                        pinc(ins, 0)
                    if v >= 2:
                        i = v - 2
                        e.wait_ge(SEM[4], bE + i + 1)
                        pinc(e.matmul(ps_all[0:64, 2560:2688], c_o1x64[:],
                                      lrec[0:1, 0:128],
                                      start=True, stop=True), 6)
                    if 1 <= v <= 17:
                        i = v - 1
                        e.wait_ge(SEM[2], bC + i + 1)
                        if i >= 1:
                            e.wait_ge(SEM[5], bF + i)
                        for k5 in range(5):
                            ins = e.matmul(
                                ps_all[0:65, 2048:2048 + 128],
                                vtl[h][:, i + k5, :],
                                ptl[:, i % 2, 128 * k5:128 * k5 + 128],
                                start=(k5 == 0), stop=(k5 == 4))
                        pinc(ins, 3)

            def la_s(e, h=h, hb=hb, bA=bA, bD=bD):
                for i in range(17):
                    e.wait_ge(SEM[0], bA + i + 1)
                    if i >= 2:
                        e.wait_ge(SEM[3], bD + i - 1)
                    sb0 = 1024 * (i % 2)
                    bsrc = ngcall[:, i:i + 1] if hb == 0 \
                        else invb[:, 100 + i:101 + i]
                    pinc(e.activation(ptl[:, i % 2, 0:640],
                                      ps_all[:, sb0:sb0 + 640], AF.Exp,
                                      bias=bsrc, scale=8.0), 1)

            def la_m(e, bB=bB):
                for i in range(17):
                    e.wait_ge(SEM[1], bB + i + 1)
                    par = i % 2
                    e.tensor_tensor(ptl[:, par, 512:640], ptl[:, par, 512:640],
                                    c_trilo[:], OP.mult)
                    ins = e.tensor_tensor(ptl[:, par, 0:128],
                                          ptl[:, par, 0:128],
                                          c_trihi[:], OP.mult)
                    if 1 <= i <= 4:
                        for k5 in range(0, 5 - i):
                            ins = e.tensor_tensor(
                                ptl[:, par, 128 * k5:128 * k5 + 128],
                                ptl[:, par, 128 * k5:128 * k5 + 128],
                                c_fw01[:], OP.mult)
                    pinc(ins, 2)

            def la_v(e, h=h, ch=ch, hb=hb, bD=bD, bG6=bG6):
                for i in range(17):
                    e.wait_ge(SEM[3], bD + i + 1)
                    e.tensor_scalar(lrow[0:1, 0:128],
                                    ps_all[64:65, 2048:2176], 1e-30, None,
                                    op0=OP.add)
                    pinc(e.reciprocal(lrec[0:1, 0:128], lrow[0:1, 0:128]), 4)
                    e.wait_ge(SEM[6], bG6 + i + 1)
                    e.tensor_copy(obc[:, 0:128], ps_all[0:64, 2560:2688])
                    pinc(e.tensor_tensor(
                        o_loc[64 * hb:64 * hb + 64, ch,
                              128 * i:128 * i + 128],
                        ps_all[0:64, 2048:2176], obc[:, 0:128], OP.mult), 5)
            group(("tensor", la_t), ("scalar", la_s), ("gpsimd", la_m),
                  ("vector", la_v))

        if DEBUG and l == 0:
            def pdol(e):
                dma(e, "dsy", dbg["oloc"], o_loc[:])
                dwait(e, "dsy")
            sec("sync", pdol)

        LA.close()
        if TRUNCV == 6 and l == 0:
            trunc_fin(o_loc[:, 0, 0:2048], 2048, stacks=(B,))
            return nc

        LB = ExitStack()
        ELB = LB.enter_context
        wof_bf2 = ELB(nc.sbuf_tensor(f"wof2_{l}", [128, 4, 1024], BF16))
        xw8 = ELB(nc.sbuf_tensor(f"xw8_{l}", [128, 8, 513], F32))
        evp_b = ELB(nc.sbuf_tensor(f"evp_b{l}", [128, 8, 512], F32))

        src_dram = xn_dram if l == 0 else xcA
        tgt_dram = xcA if l == 0 else xcB

        def lb_w(e, l=l):
            dma(e, "dgp", wof_bf2[:], wof[l].rearrange("(c p) f -> p c f",
                                                       p=128))
            dwait(e, "dgp")
        sec("gpsimd", lb_w)

        for k4 in range(4):
            bA = scnt[0]; bH = scnt[7]

            def wo_ld(e, k4=k4, src_dram=src_dram):
                dma(e, "dsy", evp_b[:], src_dram[:, :, 1 + 512 * k4:
                                                 513 + 512 * k4])
                dwait(e, "dsy")

            def wo_t(e, k4=k4, bH=bH):
                for fc in range(8):
                    a = 512 * (fc % 4)
                    if fc >= 4:
                        e.wait_ge(SEM[7], bH + fc - 3)
                    for ec in range(4):
                        ins = e.matmul(PS(0, 2048)[:, a:a + 512],
                                       wof_bf2[:, ec, 128 * fc:128 * fc + 128],
                                       o_loc[:, ec, 128 + 512 * k4:
                                             640 + 512 * k4],
                                       start=(ec == 0), stop=(ec == 3))
                    pinc(ins, 0)

            def wo_v(e, k4=k4, bA=bA):
                dwait(e, "dsy")
                for fc in range(8):
                    a = 512 * (fc % 4)
                    e.wait_ge(SEM[0], bA + fc + 1)
                    pinc(e.tensor_tensor(evp_b[:, fc, :],
                                         PS(0, 2048)[:, a:a + 512],
                                         evp_b[:, fc, :], OP.add), 7)

            def wo_st(e, k4=k4, bH=bH, tgt_dram=tgt_dram):
                e.wait_ge(SEM[7], bH + 8)
                dma(e, "dsy", tgt_dram[:, :, 1 + 512 * k4:513 + 512 * k4],
                    evp_b[:])
                dwait(e, "dsy")
            group(("sync", wo_ld), ("tensor", wo_t), ("vector", wo_v))
            sec("sync", wo_st)

        bA = scnt[0]
        bH = scnt[7]

        def woe_t(e, bH=bH):
            for fc in range(8):
                a = 512 * (fc % 4)
                if fc >= 4:
                    e.wait_ge(SEM[7], bH + fc - 3)
                for ec in range(4):
                    ins = e.matmul(PS(0, 2048)[:, a:a + 128],
                                   wof_bf2[:, ec, 128 * fc:128 * fc + 128],
                                   o_loc[:, ec, 0:128],
                                   start=(ec == 0), stop=(ec == 3))
                pinc(ins, 0)

        def woe_v(e, l=l, bA=bA):
            for fc in range(8):
                a = 512 * (fc % 4)
                e.wait_ge(SEM[0], bA + fc + 1)
                if l == 0:
                    ins = e.tensor_tensor(mid_x[:, fc, :],
                                          PS(0, 2048)[:, a:a + 128],
                                          xn_extra[:, fc, :], OP.add)
                else:
                    ins = e.tensor_tensor(mid_x[:, fc, :],
                                          PS(0, 2048)[:, a:a + 128],
                                          xqin[:, fc, 520:648], OP.add)
                pinc(ins, 7)
        group(("tensor", woe_t), ("vector", woe_v))

        def woe_st(e, tgt_dram=tgt_dram):
            dma_nc(e, "dsy", tgt_dram[:, :, 0:1], mid_x[:, :, 127:128])
            dwait(e, "dsy")
        sec("sync", woe_st)

        LC = ExitStack()
        hshT = LC.enter_context(nc.sbuf_tensor(f"hsh{l}", [128, 8, 513], BF16))
        hbf = LC.enter_context(nc.sbuf_tensor(f"hbf{l}", [128, 8, LO], BF16))

        for t2 in range(4):
            def f_cast(e, t2=t2, tgt_dram=tgt_dram):
                dma(e, "dgp", hshT[:], tgt_dram[:, :, 512 * t2:512 * t2 + 513])
                dwait(e, "dgp")
            sec("gpsimd", f_cast)

            def f_ldx(e, t2=t2, tgt_dram=tgt_dram):
                dma(e, "dsy", xw8[:], tgt_dram[:, :, 512 * t2:512 * t2 + 513])
                dwait(e, "dsy")
            sec("sync", f_ldx)

            bS = scnt[1]
            bT = scnt[0]

            def f_sq(e, bT=bT):
                for ec in range(8):
                    o = 1 if ec < 4 else 0
                    if ec >= 2:
                        e.wait_ge(SEM[0], bT + ec - 1)
                    par = sqst if ec % 2 == 0 else vstage
                    pinc(e.activation(par[:], hshT[:, ec, o:o + 512],
                                      AF.Square, bias=0.0, scale=1.0), 1)

            def f_mm(e, bS=bS):
                for ec in range(8):
                    o = 1 if ec < 4 else 0
                    e.matmul(PS(1024, 1536)[0:1, 0:512], c_ones128[:],
                             hshT[:, ec, o:o + 512], start=(ec == 0),
                             stop=(ec == 7))
                for ec in range(8):
                    e.wait_ge(SEM[1], bS + ec + 1)
                    par = sqst if ec % 2 == 0 else vstage
                    pinc(e.matmul(PS(1536, 2048)[0:1, 0:512], c_ones128[:],
                                  par[:], start=(ec == 0), stop=(ec == 7)), 0)
            group(("scalar", f_sq), ("tensor", f_mm))

            def f_st(e):
                e.tensor_scalar(stat1[:], PS(1024, 1536)[0:1, 0:512], 1.0 / D_,
                                None, op0=OP.mult)
                e.tensor_tensor(stat2[:], stat1[:], stat1[:], OP.mult)
                e.tensor_scalar(stat2[:], stat2[:], -1e-5, None, op0=OP.add)
                e.scalar_tensor_tensor(stat3[:], PS(1536, 2048)[0:1, 0:512],
                                       1.0 / D_, stat2[:], op0=OP.mult,
                                       op1=OP.subtract)
            sec("vector", f_st)
            sec("scalar", lambda e: e.activation(stat3[:], stat3[:], AF.Sqrt,
                                                 bias=0.0, scale=1.0))

            def f_inv(e):
                e.reciprocal(stat2[:], stat3[:])
                e.tensor_tensor(stat1[:], stat1[:], stat2[:], OP.mult)
                e.tensor_scalar(stat1[:], stat1[:], -1.0, None, op0=OP.mult)
            sec("vector", f_inv)

            def f_bc(e):
                e.matmul(PS(2048, 2560)[:, 0:512], c_o1x128[:], stat2[:],
                         start=True, stop=True)
                e.matmul(PS(2560, 3072)[:, 0:512], c_o1x128[:], stat1[:],
                         start=True, stop=True)
            sec("tensor", f_bc)

            def f_bcc(e):
                e.tensor_copy(invb[:], PS(2048, 2560)[:, 0:512])
                e.tensor_copy(invb2[:], PS(2560, 3072)[:, 0:512])
            sec("vector", f_bcc)

            bV = scnt[2]
            bW3 = scnt[3]

            def f_a(e, t2=t2, bW3=bW3):
                for ec in range(8):
                    o = 1 if ec < 4 else 0
                    if ec >= 2:
                        e.wait_ge(SEM[3], bW3 + ec - 1)
                    dstp = ev32 if ec % 2 == 0 else ev32b
                    e.tensor_tensor(dstp[:], xw8[:, ec, o:o + 512],
                                    invb[:], OP.mult)
                    pinc(e.tensor_tensor(dstp[:], dstp[:], invb2[:], OP.add),
                         2)

            def f_h(e, l=l, t2=t2, bV=bV):
                last = None
                for ec in range(8):
                    e.wait_ge(SEM[2], bV + ec + 1)
                    srcp = ev32 if ec % 2 == 0 else ev32b
                    last = e.tensor_scalar(
                        hbf[:, ec, 512 * t2:512 * t2 + 512], srcp[:],
                        c_lg[:, 8 * l + ec:8 * l + ec + 1],
                        c_lb[:, 8 * l + ec:8 * l + ec + 1],
                        op0=OP.mult, op1=OP.add)
                    pinc(last, 3)
                return None
            group(("vector", f_a), ("gpsimd", f_h))

        w1s = LC.enter_context(nc.sbuf_tensor(f"w1s{l}", [128, 2, 2, 8, 128],
                                              BF16))
        w2s = LC.enter_context(nc.sbuf_tensor(f"w2s{l}", [128, 2, 512], BF16))
        vgl = big_bf[:, 0:11264].rearrange("p (c t) -> p c t", c=22)

        for sub2 in range(2):
            for q2 in (0, 1):
                bT = scnt[0]; bS = scnt[1]; bV = scnt[7]
                dsnap = []

                def f4_g(e, l=l, bT=bT, dsnap=dsnap):
                    for fc in range(22):
                        if fc >= 2:
                            e.wait_ge(SEM[0], bT + fc - 1)
                        par = fc % 2
                        dma(e, "dgp", w1s[:, par, 0, :, 0:FCS[fc]],
                            w1a[l][:, FCO[fc]:FCO[fc] + FCS[fc]].rearrange(
                                "(c p) f -> p c f", p=128), ls=par)
                        dma(e, "dgp", w1s[:, par, 1, :, 0:FCS[fc]],
                            w1g[l][:, FCO[fc]:FCO[fc] + FCS[fc]].rearrange(
                                "(c p) f -> p c f", p=128), ls=2 + par)
                        dsnap.append((lcnt[par], lcnt[2 + par]))

                def f4_t(e, sub2=sub2, q2=q2, bV=bV, dsnap=dsnap):
                    t0 = 1024 * sub2 + 512 * q2
                    for fc in range(22):
                        par = fc % 2
                        e.wait_ge(LS[par], dsnap[fc][0])
                        e.wait_ge(LS[2 + par], dsnap[fc][1])
                        if fc >= 1:
                            e.wait_ge(SEM[7], bV + fc)
                        for ag in (0, 1):
                            bank = 1024 * ag
                            for ec in range(8):
                                ins = e.matmul(
                                    ps_all[0:FCS[fc], bank:bank + 512],
                                    w1s[:, par, ag, ec, 0:FCS[fc]],
                                    hbf[:, ec, t0:t0 + 512],
                                    start=(ec == 0), stop=(ec == 7))
                        pinc(ins, 0)

                def f4_s(e, l=l, bT=bT, bV=bV):
                    for fc in range(22):
                        e.wait_ge(SEM[0], bT + fc + 1)
                        if fc >= 2:
                            e.wait_ge(SEM[7], bV + fc - 1)
                        pinc(e.activation(
                            gel[0:FCS[fc], fc % 2, :],
                            ps_all[0:FCS[fc], 1024:1536], AF.Gelu,
                            bias=c_b1g[0:FCS[fc],
                                       22 * l + fc:22 * l + fc + 1],
                            scale=1.0), 1)

                def f4_v(e, l=l, bS=bS):
                    for fc in range(22):
                        e.wait_ge(SEM[1], bS + fc + 1)
                        e.tensor_scalar(ev32[0:FCS[fc], :],
                                        ps_all[0:FCS[fc], 0:512],
                                        c_b1a[0:FCS[fc],
                                              22 * l + fc:22 * l + fc + 1],
                                        None, op0=OP.add)
                        pinc(e.tensor_tensor(
                            vgl[0:FCS[fc], fc, :],
                            ev32[0:FCS[fc], :], gel[0:FCS[fc], fc % 2, :],
                            OP.mult), 7)
                group(("gpsimd", f4_g), ("tensor", f4_t), ("scalar", f4_s),
                      ("vector", f4_v))

                def w2_xld(e, sub2=sub2, q2=q2, tgt_dram=tgt_dram):
                    t0 = 1 + 1024 * sub2 + 512 * q2
                    dma(e, "dsy", xw8[:, :, 0:512],
                        tgt_dram[:, :, t0:t0 + 512])
                    dwait(e, "dsy")
                sec("sync", w2_xld)

                for grp in (0, 1):
                    bT = scnt[0]
                    dsnap2 = []

                    def w2_g(e, l=l, grp=grp, bT=bT, dsnap2=dsnap2):
                        for fc in range(22):
                            if fc >= 2:
                                e.wait_ge(SEM[0], bT + fc - 1)
                            dma(e, "dgp", w2s[0:FCS[fc], fc % 2, :],
                                w2f[l][FCO[fc]:FCO[fc] + FCS[fc],
                                       512 * grp:512 * grp + 512],
                                ls=4 + fc % 2)
                            dsnap2.append(lcnt[4 + fc % 2])

                    def w2_t(e, q2=q2, bT=bT, dsnap2=dsnap2):
                        for fc in range(22):
                            e.wait_ge(LS[4 + fc % 2], dsnap2[fc])
                            for j4 in range(4):
                                ins = e.matmul(
                                    ps_all[:, 512 * j4:512 * j4 + 512],
                                    w2s[0:FCS[fc], fc % 2,
                                        128 * j4:128 * j4 + 128],
                                    vgl[0:FCS[fc], fc, :],
                                    start=(fc == 0), stop=(fc == 21))
                            pinc(ins, 0)

                    def w2_v(e, l=l, grp=grp, bT=bT):
                        e.wait_ge(SEM[0], bT + 22)
                        for j4 in range(4):
                            f2c = 4 * grp + j4
                            e.scalar_tensor_tensor(
                                xw8[:, f2c, 0:512],
                                ps_all[:, 512 * j4:512 * j4 + 512],
                                c_b2[:, 8 * l + f2c:8 * l + f2c + 1],
                                xw8[:, f2c, 0:512],
                                op0=OP.add, op1=OP.add)
                    group(("gpsimd", w2_g), ("tensor", w2_t),
                          ("vector", w2_v))

                def w2_st(e, sub2=sub2, q2=q2, tgt_dram=tgt_dram):
                    t0 = 1 + 1024 * sub2 + 512 * q2
                    dma(e, "dsy", tgt_dram[:, :, t0:t0 + 512],
                        xw8[:, :, 0:512])
                    dwait(e, "dsy")
                sec("sync", w2_st)

        LC.close()

        if DEBUG and l == 0:
            def pdx1(e):
                dma(e, "dsy", dbg["x1"], xcA[:, :, :])
                dwait(e, "dsy")
            sec("sync", pdx1)

        if l == 0:
            LT = ExitStack()
            tail = LT.enter_context(nc.sbuf_tensor("tail", [128, 8, HALO],
                                                   BF16))
            slot = LT.enter_context(nc.sbuf_tensor("slot", [128, 8, HALO],
                                                   BF16))

            def ex_c(e):
                dma(e, "dgp", tail[:], xcA[:, :, 2049 - HALO:2049])
                dwait(e, "dgp")
            sec("gpsimd", ex_c)

            for r in range(4):
                def ex_m(e, r=r):
                    e.tensor_scalar(slot[:].rearrange("p c t -> p (c t)"),
                                    tail[:].rearrange("p c t -> p (c t)"),
                                    c_sel[:, r:r + 1], None, op0=OP.mult)
                sec("vector", ex_m)

                def ex_s(e, r=r):
                    dma(e, "dsy",
                        cc_f_in[1024 * r:1024 * r + 1024, :].rearrange(
                            "(c p) t -> p c t", p=128), slot[:])
                    dwait(e, "dsy")
                sec("sync", ex_s)

            def ex_cc(e):
                e.collective_compute(
                    "ReduceScatter", OP.add,
                    replica_groups=[[0, 1, 2, 3], [4, 5, 6, 7]],
                    ins=[cc_f_in.ap().opt()], outs=[cc_f_out.ap().opt()],
                ).then_inc(SEM[6])
                scnt[6] += 1
                e.wait_ge(SEM[6], scnt[6])
            sec("gpsimd", ex_cc)

            def ex_b(e):
                dma(e, "dgp", xqin[:, :, 0:HALO],
                    cc_f_out[:, :].rearrange("(c p) t -> p c t", p=128))
                dma(e, "dgp", xqin[:, :, HALO:HALO + LO],
                    xcA[:, :, 1:2049])
                dwait(e, "dgp")
            sec("gpsimd", ex_b)
            LT.close()

        LB.close()

    if TRUNCV == 7:
        trunc_fin(big_bf[:, 0:2048], 2048, stacks=(B,))
        return nc

    B6 = ExitStack()
    f6a = B6.enter_context(nc.sbuf_tensor("f6a", [128, 8, 512], F32))
    f6b = B6.enter_context(nc.sbuf_tensor("f6b", [128, 8, 512], F32))
    f6c = B6.enter_context(nc.sbuf_tensor("f6c", [128, 8, 512], F32))

    for k4 in range(4):
        def p6_ld(e, k4=k4):
            if k4 == 0:
                e.wait_ge(ccp, 1)
            dma(e, "dsy", f6a[:],
                x_loc[:, 1024 + 512 * k4:1536 + 512 * k4].rearrange(
                    "(c p) t -> p c t", p=128))
            dma(e, "dsy", f6b[:],
                cc_p_out[:, 512 * k4:512 * k4 + 512].rearrange(
                    "(c p) t -> p c t", p=128))
            dma(e, "dsy", f6c[:], xcB[:, :, 1 + 512 * k4:513 + 512 * k4])
            dwait(e, "dsy")
        sec("sync", p6_ld)

        def p6_v(e):
            e.tensor_tensor(f6a[:].rearrange("p c t -> p (c t)"),
                            f6a[:].rearrange("p c t -> p (c t)"),
                            f6b[:].rearrange("p c t -> p (c t)"), OP.add)
            e.tensor_tensor(f6a[:].rearrange("p c t -> p (c t)"),
                            f6a[:].rearrange("p c t -> p (c t)"),
                            f6c[:].rearrange("p c t -> p (c t)"), OP.add)
        sec("vector", p6_v)

        def p6_st(e, k4=k4):
            dma(e, "dsy",
                y_out[:, 512 * k4:512 * k4 + 512].rearrange(
                    "(c p) t -> p c t", p=128), f6a[:])
            dwait(e, "dsy")
        sec("sync", p6_st)

    if DEBUG:
        def pdpr(e):
            dma(e, "dsy", dbg["proj"], cc_p_out[:, :])
            dwait(e, "dsy")
        sec("sync", pdpr)

    B6.close()
    B.close()
    outer.close()

    def _cov(clock, keys, tgt, need):
        if clock.get(tgt, -1) >= need:
            return True
        for key in keys:
            c = pincby.get(key)
            if c is None and _P1PB is not None and key[0] is not None:
                c = _P1PB.get(key)
            if c is not None and c.get(tgt, -1) >= need:
                return True
        return False

    real = {}
    for pC, pun, my_eng, my_idx, loc in defA:
        if not _cov(pC, pun, my_eng, my_idx):
            real[loc] = real.get(loc, 0) + 1
    for mC, mun, my_ei, my_eng, pqn, pcnt, pC, pun, loc in defB:
        if not (_cov(mC, mun, pqn, pcnt)
                or _cov(pC, pun, my_eng, my_ei + 1)):
            real[loc] = real.get(loc, 0) + 1
    import sys
    for loc, cnt in sorted(real.items()):
        print(f"AUTOSYNC REAL DMA HAZARD: {loc} x{cnt}", file=sys.stderr)
    if "barloc" in autostats:
        top = sorted(autostats["barloc"].items(), key=lambda kv: -kv[1][0])
        for (ln, fn2), c in top[:25]:
            print(f"BAR {fn2}@{ln} x{c[0]}", file=sys.stderr)
    globals()["_LAST_PINCBY"] = dict(pincby)
    print(f"AUTOSYNC: bars={autostats['bar']} realdma={sum(real.values())} "
          f"defA={len(defA)} defB={len(defB)} acnt={acnt} "
          f"chain={st['chain']} dsy={st['dsy']} dgp={st['dgp']} "
          f"scnt={scnt}", file=sys.stderr)
    return nc



def _host_tables():
    pos = np.arange(S_, dtype=np.float64)
    inv_freq = 1.0 / 10000.0 ** (np.arange(0, 32, 2, dtype=np.float64) / 32.0)
    sinus = pos[:, None] * inv_freq[None, :]
    sn, cs = np.sin(sinus), np.cos(sinus)
    C = np.zeros((128, S_), np.float32)
    Sg = np.zeros((128, S_), np.float32)
    C[32:64] = 1.0
    for i in range(16):
        C[2 * i] = cs[:, i]
        C[2 * i + 1] = cs[:, i]
        Sg[2 * i] = -sn[:, i]
        Sg[2 * i + 1] = sn[:, i]
    C[64:128] = C[0:64]
    Sg[64:128] = Sg[0:64]
    return C, Sg


def _xpos_tables(s0):
    t = np.arange(s0 - HALO, s0 + LO, dtype=np.float64)
    half = np.arange(0, 64, 2, dtype=np.float64)
    inv_f = 1.0 / 10000.0 ** (half / 64.0)
    fr = t[:, None] * inv_f[None, :]
    fr = np.concatenate([fr, fr], axis=1)
    base = (half + 0.4 * 64.0) / (1.4 * 64.0)
    base = np.concatenate([base, base])
    power = (t - S_ // 2) / 256.0
    scale = base[None, :] ** power[:, None]
    cs, sn = np.cos(fr), np.sin(fr)

    def mk(c_t, s_t):
        A64 = np.empty((64, c_t.shape[0]), np.float32)
        B64 = np.empty((64, c_t.shape[0]), np.float32)
        cT = c_t.T.astype(np.float32)
        sT = s_t.T.astype(np.float32)
        for j in range(32):
            A64[2 * j] = cT[j]
            A64[2 * j + 1] = cT[j + 32]
            B64[2 * j] = -sT[j]
            B64[2 * j + 1] = sT[j + 32]
        return (np.concatenate([A64, A64], 0),
                np.concatenate([B64, B64], 0))
    Aq, Bq = mk(cs * scale, sn * scale)
    Ak, Bk = mk(cs / scale, sn / scale)
    return Aq, Bq, Ak, Bk


PERM64 = np.empty(64, np.int64)
PERM64[0::2] = np.arange(32)
PERM64[1::2] = np.arange(32, 64)


def _permute_qk_cols(w):
    w = w.copy()
    for base in (0, 512):
        blk = w[:, base:base + 512].reshape(D_, 8, 64)
        w[:, base:base + 512] = blk[:, :, PERM64].reshape(D_, 512)
    return np.ascontiguousarray(w)


def kernel(L0_attn_hidden_states, L0_attn_w_qkv, L0_attn_w_out, L0_ffn_wqkv,
           L0_ffn_q_scale, L0_ffn_k_scale, L0_ffn_wo, L0_ffn_ln_g,
           L0_ffn_ln_b, L0_ffn_w1, L0_ffn_b1, L0_ffn_w2, L0_ffn_b2):
    x = np.asarray(L0_attn_hidden_states, np.float32)
    wqkv = np.asarray(L0_attn_w_qkv, np.float32)
    wout = np.asarray(L0_attn_w_out, np.float32)
    fwqkv = np.asarray(L0_ffn_wqkv, np.float32)
    fqs = np.asarray(L0_ffn_q_scale, np.float32)
    fks = np.asarray(L0_ffn_k_scale, np.float32)
    fwo = np.asarray(L0_ffn_wo, np.float32)
    flg = np.asarray(L0_ffn_ln_g, np.float32)
    flb = np.asarray(L0_ffn_ln_b, np.float32)
    fw1 = np.asarray(L0_ffn_w1, np.float32)
    fb1 = np.asarray(L0_ffn_b1, np.float32)
    fw2 = np.asarray(L0_ffn_w2, np.float32)
    fb2 = np.asarray(L0_ffn_b2, np.float32)

    C32, S32 = _host_tables()
    woT_full = np.ascontiguousarray(wout.T)

    tri_lo = np.tril(np.ones((128, 128), np.float32)).T.astype(BFD)
    tri_hi = np.tril(np.ones((128, 128), np.float32)).astype(BFD)

    in_maps = []
    for c in range(8):
        b, g = c // 4, c % 4
        s0 = 2048 * g
        xbT = np.ascontiguousarray(x[b].T)
        x_loc = np.zeros((D_, 3072), np.float32)
        lo_tok = s0 - 1024
        j0 = max(0, -lo_tok)
        x_loc[:, j0:] = xbT[:, lo_tok + j0:s0 + LO]
        wq = wqkv[768 * g:768 * g + 256]
        wv = wqkv[768 * g + 256:768 * g + 512]
        wk = wqkv[768 * g + 512:768 * g + 768]
        watt = np.ascontiguousarray(np.concatenate([wq, wk, wv], 0).T)
        woT = np.ascontiguousarray(woT_full[256 * g:256 * g + 256])
        Aq, Bq, Ak, Bk = _xpos_tables(s0)

        qs_c = np.zeros((1, 128), np.float32)
        ks_c = np.zeros((1, 128), np.float32)
        lg_c = np.zeros((128, 16), np.float32)
        lb_c = np.zeros((128, 16), np.float32)
        b1a_c = np.zeros((128, 44), np.float32)
        b1g_c = np.zeros((128, 44), np.float32)
        b2_c = np.zeros((128, 16), np.float32)
        for l in range(2):
            qs_c[0, 64 * l:64 * l + 64] = fqs[l][PERM64]
            ks_c[0, 64 * l:64 * l + 64] = fks[l][PERM64]
            for ec in range(8):
                lg_c[:, 8 * l + ec] = flg[l, 128 * ec:128 * ec + 128]
                lb_c[:, 8 * l + ec] = flb[l, 128 * ec:128 * ec + 128]
                b2_c[:, 8 * l + ec] = fb2[l, 128 * ec:128 * ec + 128]
            for fc in range(22):
                b1a_c[0:FCS[fc], 22 * l + fc] = \
                    fb1[l, FCO[fc]:FCO[fc] + FCS[fc]]
                b1g_c[0:FCS[fc], 22 * l + fc] = \
                    fb1[l, HID + FCO[fc]:HID + FCO[fc] + FCS[fc]]

        sel = np.zeros((128, 4), np.float32)
        if g < 3:
            sel[:, g + 1] = 1.0
        indh2 = np.zeros((128, 2), np.float32)
        indh2[0:64, 0] = 1.0
        indh2[64:128, 1] = 1.0

        m = dict(
            x_fm=xbT, x_loc=x_loc, watt=watt, woT=woT, ropc=C32, rops=S32,
            wqkvf0=_permute_qk_cols(fwqkv[0]),
            wqkvf1=_permute_qk_cols(fwqkv[1]),
            wof0=np.ascontiguousarray(fwo[0]),
            wof1=np.ascontiguousarray(fwo[1]),
            w1a0=np.ascontiguousarray(fw1[0][:, :HID]),
            w1a1=np.ascontiguousarray(fw1[1][:, :HID]),
            w1g0=np.ascontiguousarray(fw1[0][:, HID:]),
            w1g1=np.ascontiguousarray(fw1[1][:, HID:]),
            w2f0=np.ascontiguousarray(fw2[0]),
            w2f1=np.ascontiguousarray(fw2[1]),
            qs_col=qs_c, ks_col=ks_c, lg_col=lg_c, lb_col=lb_c,
            b1a_col=b1a_c, b1g_col=b1g_c, b2_col=b2_c,
            xpAq=Aq, xpBq=Bq, xpAk=Ak, xpBk=Bk,
            tri_lo=tri_lo, tri_hi=tri_hi,
            fw01=(np.ones if g > 0 else np.zeros)((128, 128)).astype(BFD),
            sel_cols=sel,
            ones128_bf=np.ones((128, 1), BFD),
            ones64_bf=np.ones((64, 1), BFD),
            ones1x128=np.ones((1, 128), np.float32),
            ones1x64=np.ones((1, 64), np.float32),
            indh2_bf=indh2.astype(BFD),
            ident_bf=np.vstack([np.eye(64), np.eye(64)]).astype(BFD),
            ident_f32=np.vstack([np.eye(64), np.eye(64)]).astype(np.float32),
        )
        in_maps.append(m)

    globals()["_PASS1_PINCBY"] = None
    build()
    globals()["_PASS1_PINCBY"] = globals().get("_LAST_PINCBY")
    nc = build()
    globals()["_PASS1_PINCBY"] = None
    res = run_bass_kernel_spmd(nc, in_maps, core_ids=list(range(8)))
    y = np.empty((B_, S_, D_), np.float32)
    for c in range(8):
        b, g = c // 4, c % 4
        y[b, 2048 * g:2048 * g + 2048, :] = \
            np.asarray(res.results[c]["y"], np.float32).T
    if DEBUG:
        kernel._dbg = [res.results[c] for c in range(8)]
    return y


# version 08 (final)
# speedup vs baseline: 1.0483x; 1.0483x over previous
import numpy as np
import ml_dtypes
import concourse.bass as bass
import concourse.mybir as mybir
from concourse.bass_utils import run_bass_kernel_spmd
from concourse.alu_op_type import AluOpType as OP
from contextlib import ExitStack

F32 = mybir.dt.float32
BF16 = mybir.dt.bfloat16
AF = mybir.ActivationFunctionType
BFD = ml_dtypes.bfloat16

DEBUG = 0

B_, S_, D_ = 2, 8192, 1024
NT1 = 16
LQ = 2688
NKC = 21
LO = 2048
HALO = 648
HID = 2730
FCS = [128] * 21 + [42]
FCO = [sum(FCS[:i]) for i in range(22)]
SWAP32 = [i ^ 1 for i in range(32)]
QT = [(o, min(512, LQ - o)) for o in range(0, LQ, 512)]


def build():
    nc = bass.Bass("TRN2", target_bir_lowering=False, debug=False)

    def I(name, shape, dt=F32):
        return nc.dram_tensor(name, list(shape), dt, kind="ExternalInput").ap()

    def Oe(name, shape, dt=F32):
        return nc.dram_tensor(name, list(shape), dt, kind="ExternalOutput").ap()

    x_fm = I("x_fm", [D_, S_])
    x_loc = I("x_loc", [D_, 3072])
    watt = I("watt", [D_, 768])
    woT = I("woT", [256, D_])
    ropc = I("ropc", [128, S_])
    rops = I("rops", [128, S_])
    wqkvf = [I(f"wqkvf{l}", [D_, 1536]) for l in range(2)]
    wof = [I(f"wof{l}", [512, D_]) for l in range(2)]
    w1a = [I(f"w1a{l}", [D_, HID]) for l in range(2)]
    w1g = [I(f"w1g{l}", [D_, HID]) for l in range(2)]
    w2f = [I(f"w2f{l}", [HID, D_]) for l in range(2)]
    qs_col = I("qs_col", [1, 128])
    ks_col = I("ks_col", [1, 128])
    lg_col = I("lg_col", [128, 16])
    lb_col = I("lb_col", [128, 16])
    b1a_col = I("b1a_col", [128, 44])
    b1g_col = I("b1g_col", [128, 44])
    b2_col = I("b2_col", [128, 16])
    xpAq = I("xpAq", [128, LQ + 8])
    xpBq = I("xpBq", [128, LQ + 8])
    xpAk = I("xpAk", [128, LQ + 8])
    xpBk = I("xpBk", [128, LQ + 8])
    tri_lo = I("tri_lo", [128, 128], BF16)
    tri_hi = I("tri_hi", [128, 128], BF16)
    fw01 = I("fw01", [128, 128], BF16)
    sel_cols = I("sel_cols", [128, 4])
    ones128_bf = I("ones128_bf", [128, 1], BF16)
    ones64_bf = I("ones64_bf", [64, 1], BF16)
    ones1x128 = I("ones1x128", [1, 128])
    ones1x64 = I("ones1x64", [1, 64])
    indh2_bf = I("indh2_bf", [128, 2], BF16)
    ident_bf = I("ident_bf", [128, 64], BF16)
    ident_f32 = I("ident_f32", [128, 64])

    y_out = Oe("y", [D_, LO])
    dbg = {}
    if DEBUG:
        dbg["q"] = Oe("dbg_q", [128, 2, S_], BF16)
        dbg["k"] = Oe("dbg_k", [128, 2, S_], BF16)
        dbg["onorm"] = Oe("dbg_onorm", [128, 2, S_], BF16)
        dbg["proj"] = Oe("dbg_proj", [D_, LO])
        dbg["xn"] = Oe("dbg_xn", [128, 8, LQ + 8], BF16)
        dbg["qh"] = Oe("dbg_qh", [128, 4, LQ], BF16)
        dbg["kh"] = Oe("dbg_kh", [128, 4, LQ], BF16)
        dbg["oloc"] = Oe("dbg_oloc", [128, 4, 2176], BF16)
        dbg["x1"] = Oe("dbg_x1", [128, 8, 2049])

    cc_p_in = nc.dram_tensor("cc_p_in", [4 * D_, LO], F32)
    cc_p_out = nc.dram_tensor("cc_p_out", [D_, LO], F32)
    cc_f_in = nc.dram_tensor("cc_f_in", [4 * D_, HALO], BF16)
    cc_f_out = nc.dram_tensor("cc_f_out", [D_, HALO], BF16)
    xn_dram = nc.dram_tensor("xn_dram", [128, 8, 2049], F32)
    xcA = nc.dram_tensor("xcA", [128, 8, 2049], F32)
    xcB = nc.dram_tensor("xcB", [128, 8, 2049], F32)

    outer = ExitStack()
    E = outer.enter_context

    chain = E(nc.semaphore("chain"))
    dsy_h = E(nc.semaphore("dsy"))
    dgp_h = E(nc.semaphore("dgp"))
    ccp = E(nc.semaphore("ccp"))
    SEM = [E(nc.semaphore(f"pp{i}")) for i in range(8)]
    SEMIX = {id(s): i for i, s in enumerate(SEM)}
    scnt = [0] * 8
    _P1PB = globals().get("_PASS1_PINCBY")

    ps_all = E(nc.psum_tensor("ps_all", [128, 3584], F32))

    def PS(a, b):
        return ps_all[:, a:b]

    st = {"chain": 0, "dsy": 0, "dgp": 0, "qn": None}
    hold = {}
    incof = {}

    ENGS = ("tensor", "vector", "scalar", "gpsimd", "sync")
    ASEM = {en: E(nc.semaphore(f"as_{en}")) for en in ENGS}
    acnt = {en: 0 for en in ENGS}
    LS = [E(nc.semaphore(f"ls{i}")) for i in range(6)]
    lcnt = [0] * 6
    LSID = {id(s): ("ls", i) for i, s in enumerate(LS)}
    est = {en: {"w": [], "r": [], "out": None} for en in ENGS}
    eidx = {en: 0 for en in ENGS}
    qclk = {en: {} for en in ENGS}
    insinfo = {}
    pincby = {}
    chain_cum = [{}]
    pend_dma = {"dsy": [], "dgp": []}
    unres = {en: [] for en in ENGS}
    defA = []
    defB = []
    autostats = {"bar": 0, "adw": 0, "warn": 0}

    def _merge(dst, src):
        for k, v in src.items():
            if dst.get(k, -1) < v:
                dst[k] = v

    def _join(a, b):
        d = dict(a)
        _merge(d, b)
        return d

    def _box(ap):
        t = ap.tensor
        pairs = list(ap.ap)
        off = ap.offset
        name = t.name
        if getattr(t, "base_partition", None) is not None and pairs:
            pitch = pairs[0][0]
            lo = hi = 0
            for s0, z0 in pairs[1:]:
                d = (z0 - 1) * s0
                if d < 0:
                    lo += d
                else:
                    hi += d
            if pitch <= 0 or len(pairs) == 0:
                return (name, 0, 1 << 40, 0, 1 << 40)
            p0 = off // pitch
            e0 = off % pitch
            if e0 + lo < 0 or e0 + hi >= pitch:
                return (name, 0, 1 << 40, 0, 1 << 40)
            return (name, p0, p0 + pairs[0][1] - 1, e0 + lo, e0 + hi)
        lo = hi = off
        for s0, z0 in pairs:
            d = (z0 - 1) * s0
            if d < 0:
                lo += d
            else:
                hi += d
        return (name, 0, 0, lo, hi)

    def _olap(a, b):
        return (a[0] == b[0] and a[2] >= b[1] and b[2] >= a[1]
                and a[4] >= b[3] and b[4] >= a[3])

    def _isap(v):
        return hasattr(v, "tensor") and hasattr(v, "ap") \
            and hasattr(v, "offset")

    def _barrier(raw, eng):
        s = est[eng]
        ins = s["out"]
        s["w"].clear()
        s["r"].clear()
        if ins is None:
            return
        ie, idx, cclk, _ = insinfo[id(ins)]
        if qclk[eng].get(ie, -1) >= idx:
            return
        io = incof.get(id(ins))
        if io is None:
            acnt[eng] += 1
            ins.then_inc(ASEM[eng], 1)
            io = (ASEM[eng], acnt[eng], ins)
            incof[id(ins)] = io
        raw.wait_ge(io[0], io[1])
        _merge(qclk[eng], cclk)
        if qclk[eng].get(ie, -1) < idx:
            qclk[eng][ie] = idx
        autostats["bar"] += 1
        if _os.environ.get("KBARLOG"):
            import traceback as _tb
            for fr in reversed(_tb.extract_stack()):
                if "kernel.py" in fr.filename and fr.lineno < 1700 \
                        and "_barrier" not in (fr.name or "") \
                        and fr.name not in ("w", "body", "group", "sec"):
                    autostats.setdefault("barloc", {}).setdefault(
                        (fr.lineno, fr.name), [0])[0] += 1
                    break

    def _warn(n, qn, detail=""):
        import traceback
        fr = traceback.extract_stack()[-3]
        autostats["warn"] += 1
        print(f"AUTOSYNC WARN: {n} overlaps pending {qn} DMA at "
              f"{fr.filename}:{fr.lineno} {detail}")

    class Rec:

        def __init__(self, e, eng):
            object.__setattr__(self, "_e", e)
            object.__setattr__(self, "eng", eng)
            object.__setattr__(self, "last", ("seq", None))

        def __getattr__(self, n):
            raw = object.__getattribute__(self, "_e")
            a = getattr(raw, n)
            if not callable(a):
                return a
            eng = object.__getattribute__(self, "eng")

            def w(*args, **kw):
                if n == "wait_ge":
                    sem, val = args[0], args[1]
                    if sem is chain:
                        _merge(qclk[eng],
                               chain_cum[min(val, len(chain_cum) - 1)])
                    elif sem is dsy_h or sem is dgp_h or id(sem) in LSID:
                        if id(sem) in LSID:
                            ck = LSID[id(sem)]
                        else:
                            ck = "dsy" if sem is dsy_h else "dgp"
                        for qn2 in ("dsy", "dgp"):
                            pend_dma[qn2] = [t for t in pend_dma[qn2]
                                             if t[3] != ck or t[2] > val]
                        _merge(qclk[eng], {ck: val})
                    else:
                        key = (SEMIX.get(id(sem)), val)
                        c = pincby.get(key)
                        if c is None and _P1PB is not None \
                                and key[0] is not None:
                            c = _P1PB.get(key)
                        if c is not None:
                            _merge(qclk[eng], c)
                        else:
                            unres[eng].append(key)
                    r = a(*args, **kw)
                    object.__setattr__(self, "last", ("seq", None))
                    return r
                if n in ("sem_inc", "nop"):
                    r = a(*args, **kw)
                    object.__setattr__(self, "last", ("seq", None))
                    return r
                if n == "collective_compute":
                    r = a(*args, **kw)
                    object.__setattr__(self, "last", ("dma", r))
                    return r
                wb, rb = [], []

                def add(v, dst):
                    if isinstance(v, (list, tuple)):
                        for x in v:
                            add(x, dst)
                    elif _isap(v):
                        dst.append(_box(v))
                is_dma = n in ("dma_start", "indirect_dma_start")
                if is_dma:
                    add(kw.get("out", args[0] if args else None), wb)
                    add(kw.get("in_", args[1] if len(args) > 1 else None), rb)
                else:
                    for i2, v in enumerate(args):
                        add(v, wb if i2 == 0 else rb)
                    for k2, v in kw.items():
                        add(v, wb if k2 in ("out", "outs", "accum_out")
                            else rb)
                s = est[eng]
                myclk = qclk[eng].get(eng, -1)
                s["w"] = [t for t in s["w"] if t[2] > myclk]
                s["r"] = [t for t in s["r"] if t[1] > myclk]
                haz = False
                for b in rb:
                    if any(_olap(b, pb) for pb, _, _ in s["w"]):
                        haz = True
                        break
                if not haz:
                    for b in wb:
                        if any(_olap(b, pb) and not (n == "matmul"
                                                     and pm == "matmul")
                               for pb, pm, _ in s["w"]):
                            haz = True
                            break
                        if any(_olap(b, pb) for pb, _ in s["r"]):
                            haz = True
                            break
                if haz:
                    _barrier(raw, eng)
                if rb or wb:
                    hits = []
                    for qn in ("dsy", "dgp"):
                        if is_dma and qn == st["qn"]:
                            continue
                        for ent in pend_dma[qn]:
                            if any(ent[1] and _olap(b, ent[0]) for b in rb) \
                               or any(_olap(b, ent[0]) for b in wb):
                                hits.append(ent)
                    if hits:
                        import traceback as _tb
                        fr = _tb.extract_stack()[-3 if is_dma else -2]
                        loc = f"{n}@{fr.lineno}"
                        if is_dma:
                            mC = dict(qclk[eng])
                            mun = tuple(unres[eng])
                            for ent in hits:
                                defB.append((mC, mun, eidx[eng], eng,
                                             ent[3], ent[2], ent[4],
                                             ent[5],
                                             f"{loc} pend={ent[0]}"
                                             f"@{ent[2]}"))
                        else:
                            midx = eidx[eng] + 1
                            for ent in hits:
                                defA.append((ent[4], ent[5], eng, midx,
                                             loc))
                r = a(*args, **kw)
                if is_dma:
                    st["_dmaboxes"] = (wb, rb)
                    object.__setattr__(self, "last", ("dma", r))
                elif r is not None and hasattr(r, "then_inc"):
                    eidx[eng] += 1
                    idx = eidx[eng]
                    insinfo[id(r)] = (eng, idx, dict(qclk[eng]), r)
                    s["w"].extend((b, n, idx) for b in wb)
                    s["r"].extend((b, idx) for b in rb)
                    s["out"] = r
                    object.__setattr__(self, "last", ("eng", r))
                elif wb or rb:
                    raise RuntimeError(f"op {n} returned no instruction")
                else:
                    object.__setattr__(self, "last", ("seq", None))
                return r
            return w

    def group(*pairs):
        n0 = st["chain"]
        st["chain"] += len(pairs)
        single = len(pairs) == 1
        for eng, fn in pairs:
            def body(e, fn=fn, n0=n0, eng=eng, single=single):
                if n0:
                    e.wait_ge(chain, n0)
                    _merge(qclk[eng],
                           chain_cum[min(n0, len(chain_cum) - 1)])
                rec = Rec(e, eng)
                fn(rec)
                s = est[eng]
                ins = s["out"]
                kind, lins = rec.last
                if single and kind == "eng" and lins is ins \
                        and ins is not None \
                        and incof.get(id(ins)) is None:
                    ins.then_inc(chain, 1)
                    ie, idx, cclk, _ = insinfo[id(ins)]
                    incof[id(ins)] = (chain, n0 + 1, ins)
                    pub = dict(qclk[eng])
                    _merge(pub, cclk)
                    if pub.get(ie, -1) < idx:
                        pub[ie] = idx
                    chain_cum.append(_join(chain_cum[-1], pub))
                else:
                    _barrier(e, eng)
                    e.sem_inc(chain, 1)
                    chain_cum.append(_join(chain_cum[-1], qclk[eng]))
            getattr(hold["blk"], eng)(body)

    def sec(eng, fn):
        group((eng, fn))

    dmalog = {"dsy": [], "dgp": []}
    globals()["DMALOG"] = dmalog

    def dma(e, qn, out, in_, ls=None):
        import traceback as _tb
        dmalog[qn].append(_tb.extract_stack()[-2].lineno)
        st["qn"] = qn
        ins = e.dma_start(out=out, in_=in_)
        if ls is None:
            ins.then_inc(dsy_h if qn == "dsy" else dgp_h, 16)
            st[qn] += 16
            ck, cv = qn, st[qn]
        else:
            ins.then_inc(LS[ls], 16)
            lcnt[ls] += 16
            ck, cv = ("ls", ls), lcnt[ls]
        wb, rb = st.get("_dmaboxes", ([], []))
        en = e.eng
        C = dict(qclk[en])
        un = tuple(unres[en])
        pend_dma[qn].extend([(b, True, cv, ck, C, un) for b in wb]
                            + [(b, False, cv, ck, C, un) for b in rb])
        return ins

    def dwait(e, qn):
        e.wait_ge(dsy_h if qn == "dsy" else dgp_h, st[qn])

    def dma_nc(e, qn, out, in_):
        with nc.allow_non_contiguous_dma(reason="tiny boundary column"):
            return dma(e, qn, out, in_)

    def pinc(ins, i):
        ins.then_inc(SEM[i], 1)
        scnt[i] += 1
        info = insinfo.get(id(ins))
        if info is not None:
            ie, idx, cclk, _ = info
            c = dict(cclk)
            if c.get(ie, -1) < idx:
                c[ie] = idx
            pincby[(i, scnt[i])] = c
        incof[id(ins)] = (SEM[i], scnt[i], ins)
        return ins

    import os as _os
    TRUNCV = int(_os.environ.get("KTRUNC", "99"))

    def trunc_fin(src_ap, cols, stacks=(), wait_ccp=False):
        def fin(e):
            if wait_ccp:
                e.wait_ge(ccp, 1)
            dma(e, "dgp", y_out[0:128, 0:cols], src_ap)
            dwait(e, "dgp")
        sec("gpsimd", fin)
        for s in stacks:
            s.close()
        outer.close()

    c_ones128 = E(nc.sbuf_tensor("c_ones128", [128, 1], BF16))
    c_ones64 = E(nc.sbuf_tensor("c_ones64", [64, 1], BF16))
    c_o1x128 = E(nc.sbuf_tensor("c_o1x128", [1, 128], F32))
    c_o1x64 = E(nc.sbuf_tensor("c_o1x64", [1, 64], F32))
    c_indh2 = E(nc.sbuf_tensor("c_indh2", [128, 2], BF16))
    c_ident = E(nc.sbuf_tensor("c_ident", [128, 64], BF16))
    c_id32 = E(nc.sbuf_tensor("c_id32", [128, 64], F32))
    c_trilo = E(nc.sbuf_tensor("c_trilo", [128, 128], BF16))
    c_trihi = E(nc.sbuf_tensor("c_trihi", [128, 128], BF16))
    c_fw01 = E(nc.sbuf_tensor("c_fw01", [128, 128], BF16))
    c_sel = E(nc.sbuf_tensor("c_sel", [128, 4], F32))
    c_qs = E(nc.sbuf_tensor("c_qs", [1, 128], F32))
    c_ks = E(nc.sbuf_tensor("c_ks", [1, 128], F32))
    c_lg = E(nc.sbuf_tensor("c_lg", [128, 16], F32))
    c_lb = E(nc.sbuf_tensor("c_lb", [128, 16], F32))
    c_b1a = E(nc.sbuf_tensor("c_b1a", [128, 44], F32))
    c_b1g = E(nc.sbuf_tensor("c_b1g", [128, 44], F32))
    c_b2 = E(nc.sbuf_tensor("c_b2", [128, 16], F32))
    stat1 = E(nc.sbuf_tensor("stat1", [1, 512], F32))
    stat2 = E(nc.sbuf_tensor("stat2", [1, 512], F32))
    stat3 = E(nc.sbuf_tensor("stat3", [1, 512], F32))
    mu_bf = E(nc.sbuf_tensor("mu_bf", [1, 512], BF16))
    negr_bf = E(nc.sbuf_tensor("negr_bf", [1, 768], BF16))
    invb = E(nc.sbuf_tensor("invb", [128, 512], F32))
    invb2 = E(nc.sbuf_tensor("invb2", [128, 512], F32))
    ev32 = E(nc.sbuf_tensor("ev32", [128, 512], F32))
    ev32b = E(nc.sbuf_tensor("ev32b", [128, 512], F32))
    rope1 = E(nc.sbuf_tensor("rope1", [128, 512], F32))
    rope2 = E(nc.sbuf_tensor("rope2", [128, 512], F32))
    sqst = E(nc.sbuf_tensor("sqst", [128, 512], BF16))
    vstage = E(nc.sbuf_tensor("vstage", [128, 512], BF16))
    negc = E(nc.sbuf_tensor("negc", [128, 4], F32))
    mx2 = E(nc.sbuf_tensor("mx2", [2, 36], F32))
    lrow = E(nc.sbuf_tensor("lrow", [1, 512], F32))
    lrec = E(nc.sbuf_tensor("lrec", [1, 512], F32))
    lrec2 = E(nc.sbuf_tensor("lrec2", [1, 512], F32))
    obc = E(nc.sbuf_tensor("obc", [64, 512], F32))
    xn_extra = E(nc.sbuf_tensor("xn_extra", [128, 8, 128], F32))
    mid_x = E(nc.sbuf_tensor("mid_x", [128, 8, 128], F32))
    mkw = E(nc.sbuf_tensor("mkw", [64, 24], F32))
    mk_bf = E(nc.sbuf_tensor("mk_bf", [64, 1], BF16))
    absq = E(nc.sbuf_tensor("absq", [64, 128], BF16))
    ngc1 = E(nc.sbuf_tensor("ngc1", [1, 4], F32))
    ngcall = E(nc.sbuf_tensor("ngcall", [128, 17], F32))
    gel = E(nc.sbuf_tensor("gel", [128, 2, 512], BF16))

    blk = E(nc.Block())
    hold["blk"] = blk

    A = ExitStack()
    EA = A.enter_context
    q_bf = EA(nc.sbuf_tensor("q_bf", [128, 2, S_], BF16))
    k_bf = EA(nc.sbuf_tensor("k_bf", [128, 2, S_], BF16))
    vt = [EA(nc.sbuf_tensor(f"vt{h}", [128, 64, 65], BF16)) for h in range(4)]

    A1 = ExitStack()
    E1 = A1.enter_context
    wq_bf = E1(nc.sbuf_tensor("wq_bf", [128, 8, 768], BF16))
    xbf = E1(nc.sbuf_tensor("xbf", [128, 2, 8, 512], BF16))
    xsq = E1(nc.sbuf_tensor("xsq", [128, 8, 512], BF16))
    rc32 = E1(nc.sbuf_tensor("rc32", [128, 512], F32))
    rs32 = E1(nc.sbuf_tensor("rs32", [128, 512], F32))

    def p0_sync(e):
        for dst, src in [(c_o1x128, ones1x128), (c_o1x64, ones1x64),
                         (c_sel, sel_cols), (c_qs, qs_col), (c_ks, ks_col),
                         (c_lg, lg_col), (c_lb, lb_col), (c_b1a, b1a_col),
                         (c_b1g, b1g_col), (c_b2, b2_col)]:
            dma(e, "dsy", dst[:], src)
        for dst, src in [(c_ones128, ones128_bf), (c_ones64, ones64_bf),
                         (c_indh2, indh2_bf), (c_ident, ident_bf),
                         (c_id32, ident_f32),
                         (c_trilo, tri_lo), (c_trihi, tri_hi),
                         (c_fw01, fw01)]:
            dma(e, "dsy", dst[:], src)
        dwait(e, "dsy")
    sec("sync", p0_sync)

    def p0_gp(e):
        dma(e, "dgp", wq_bf[:], watt.rearrange("(c p) f -> p c f", p=128))
        dwait(e, "dgp")
    sec("gpsimd", p0_gp)

    def p0_mm(e):
        for ec in range(8):
            for a, w in [(0, 512), (512, 256)]:
                e.matmul(PS(0, 1024)[0:1, a:a + w], c_ones128[:],
                         wq_bf[:, ec, a:a + w], start=(ec == 0), stop=(ec == 7))
    sec("tensor", p0_mm)

    def p0_v(e):
        e.tensor_scalar(negr_bf[:], PS(0, 1024)[0:1, 0:768], -1.0, None,
                        op0=OP.mult)
        for h in range(4):
            e.memset(vt[h][:, :, 64:65], 1.0)
    sec("vector", p0_v)

    if TRUNCV == 10:
        trunc_fin(wq_bf[:, 0, 0:768], 768, stacks=(A1, A))
        return nc

    KP1 = int(_os.environ.get("KP1", "9"))
    KT1N = int(_os.environ.get("KT1N", str(NT1)))

    for tt in range(KT1N):
        t0 = 512 * tt
        bu = tt % 2

        if tt == 0:
            def p1_ld0(e):
                dma(e, "dgp", xbf[:, 0],
                    x_fm[:, 0:512].rearrange("(c p) t -> p c t", p=128))
                dma(e, "dgp", rc32[:], ropc[:, 0:512])
                dma(e, "dgp", rs32[:], rops[:, 0:512])
                dwait(e, "dgp")
            sec("gpsimd", p1_ld0)

        if KP1 < 2:
            continue

        def p1_sq(e, bu=bu):
            return e.activation(xsq[:].rearrange("p c t -> p (c t)"),
                                xbf[:, bu].rearrange("p c t -> p (c t)"),
                                AF.Square, bias=0.0, scale=1.0)
        sec("scalar", p1_sq)

        def p1_sums(e, bu=bu):
            for ec in range(8):
                e.matmul(PS(1024, 1536)[0:1, 0:512], c_ones128[:],
                         xbf[:, bu, ec], start=(ec == 0), stop=(ec == 7))
            for ec in range(8):
                e.matmul(PS(1536, 2048)[0:1, 0:512], c_ones128[:],
                         xsq[:, ec], start=(ec == 0), stop=(ec == 7))
        sec("tensor", p1_sums)

        def p1_st1(e):
            e.tensor_scalar(stat1[:], PS(1024, 1536)[0:1, 0:512], 1.0 / D_,
                            None, op0=OP.mult)
            e.tensor_copy(mu_bf[:], stat1[:])
            e.tensor_tensor(stat2[:], stat1[:], stat1[:], OP.mult)
            e.tensor_scalar(stat2[:], stat2[:], -1e-5, None, op0=OP.add)
            e.scalar_tensor_tensor(stat3[:], PS(1536, 2048)[0:1, 0:512],
                                   1.0 / D_, stat2[:], op0=OP.mult,
                                   op1=OP.subtract)
        sec("vector", p1_st1)
        sec("scalar", lambda e: e.activation(stat3[:], stat3[:], AF.Sqrt,
                                             bias=0.0, scale=1.0))
        sec("vector", lambda e: e.reciprocal(stat1[:], stat3[:]))
        sec("tensor", lambda e: e.matmul(PS(2048, 2560)[:, 0:512], c_o1x128[:],
                                         stat1[:], start=True, stop=True))
        sec("vector", lambda e: e.tensor_copy(invb[:], PS(2048, 2560)[:, 0:512]))

        if KP1 < 3:
            continue

        bQ = scnt[0]
        bR = scnt[1]

        def p1_mm(e, bu=bu, bR=bR):
            for fc in range(6):
                a = 512 * (fc % 2)
                if fc >= 2:
                    e.wait_ge(SEM[1], bR + fc - 1)
                for ec in range(8):
                    e.matmul(PS(0, 1024)[:, a:a + 512],
                             wq_bf[:, ec, 128 * fc:128 * fc + 128],
                             xbf[:, bu, ec], start=(ec == 0), stop=False)
                pinc(e.matmul(PS(0, 1024)[:, a:a + 512],
                              negr_bf[0:1, 128 * fc:128 * fc + 128],
                              mu_bf[:], start=False, stop=True), 0)
            for ch in (0, 1):
                e.wait_ge(SEM[1], bR + 5 + ch)
                for hb in (0, 1):
                    for j in range(4):
                        b0 = 2048 if hb == 0 else 3072
                        ins = e.matmul(
                            ps_all[:, b0 + 64 * j:b0 + 64 * j + 64],
                            vstage[64 * hb:64 * hb + 64,
                                   128 * j:128 * j + 128],
                            c_ident[64 * hb:64 * hb + 64, :],
                            start=True, stop=True)
                pinc(ins, 0)

        def p1_ve(e, bu=bu, tt=tt, t0=t0, bQ=bQ):
            for fc in range(4):
                a = 512 * (fc % 2)
                dst = q_bf if fc < 2 else k_bf
                ch = fc % 2
                e.wait_ge(SEM[0], bQ + fc + 1)
                e.tensor_tensor(ev32[:], PS(0, 1024)[:, a:a + 512],
                                invb[:], OP.mult)
                e.stream_shuffle(rope1[:], ev32[:], mask=SWAP32)
                e.tensor_tensor(rope2[:], ev32[:], rc32[:], OP.mult)
                e.tensor_tensor(rope1[:], rope1[:], rs32[:], OP.mult)
                pinc(e.tensor_tensor(dst[:, ch, t0:t0 + 512],
                                     rope2[:], rope1[:], OP.add), 1)
            e.wait_ge(SEM[0], bQ + 5)
            pinc(e.tensor_tensor(vstage[:], PS(0, 1024)[:, 0:512],
                                 invb[:], OP.mult), 1)
            e.wait_ge(SEM[0], bQ + 7)
            for hb in (0, 1):
                for j in range(4):
                    b0 = 2048 if hb == 0 else 3072
                    e.tensor_copy(vt[0 + hb][:, 4 * tt + j, 0:64],
                                  ps_all[:, b0 + 64 * j:b0 + 64 * j + 64])
            pinc(e.tensor_tensor(vstage[:], PS(0, 1024)[:, 512:1024],
                                 invb[:], OP.mult), 1)
            e.wait_ge(SEM[0], bQ + 8)
            ins = None
            for hb in (0, 1):
                for j in range(4):
                    b0 = 2048 if hb == 0 else 3072
                    ins = e.tensor_copy(vt[2 + hb][:, 4 * tt + j, 0:64],
                                        ps_all[:, b0 + 64 * j:
                                               b0 + 64 * j + 64])
            pinc(ins, 1)
        def p1_nld(e, tt=tt, bR=bR):
            if tt + 1 < KT1N:
                t1 = 512 * (tt + 1)
                dma(e, "dgp", xbf[:, (tt + 1) % 2],
                    x_fm[:, t1:t1 + 512].rearrange("(c p) t -> p c t",
                                                   p=128))
                e.wait_ge(SEM[1], bR + 4)
                dma(e, "dgp", rc32[:], ropc[:, t1:t1 + 512])
                dma(e, "dgp", rs32[:], rops[:, t1:t1 + 512])
                dwait(e, "dgp")
        group(("tensor", p1_mm), ("vector", p1_ve), ("gpsimd", p1_nld))

    if DEBUG:
        def pd1(e):
            dma(e, "dsy", dbg["q"], q_bf[:])
            dma(e, "dsy", dbg["k"], k_bf[:])
            dwait(e, "dsy")
        sec("sync", pd1)

    if TRUNCV == 1:
        trunc_fin(q_bf[:, 0, 0:2048], 2048, stacks=(A1, A))
        return nc

    A1.close()
    A2 = ExitStack()
    EA2 = A2.enter_context
    o_norm = EA2(nc.sbuf_tensor("o_norm", [128, 2, S_], BF16))
    woT_bf = EA2(nc.sbuf_tensor("woT_bf", [128, 2, 1024], BF16))
    pt_sb = EA2(nc.sbuf_tensor("pt_sb", [128, 2, 1024], BF16))
    evp = EA2(nc.sbuf_tensor("evp", [128, 8, 512], F32))

    def pA2_gp(e):
        dma(e, "dgp", woT_bf[:], woT.rearrange("(c p) f -> p c f", p=128))
        dwait(e, "dgp")
    sec("gpsimd", pA2_gp)

    for hp in range(2):
        for mrow, buf in ((0, q_bf), (1, k_bf)):
            bA = scnt[0]
            bB = scnt[1]

            def pp_sq(e, buf=buf, hp=hp, bB=bB):
                for t in range(NT1):
                    if t >= 2:
                        e.wait_ge(SEM[1], bB + t - 1)
                    par = sqst if t % 2 == 0 else vstage
                    pinc(e.activation(par[:], buf[:, hp, 512 * t:512 * t + 512],
                                      AF.Square, bias=0.0, scale=1.0), 0)

            def pp_mm(e, bA=bA, bC2=scnt[2]):
                for t in range(NT1):
                    e.wait_ge(SEM[0], bA + t + 1)
                    if t >= 2:
                        e.wait_ge(SEM[2], bC2 + t - 1)
                    par = sqst if t % 2 == 0 else vstage
                    po = 2560 + 512 * (t % 2)
                    pinc(e.matmul(ps_all[0:2, po:po + 512], c_indh2[:], par[:],
                                  start=True, stop=True), 1)

            def pp_rd(e, bB=bB, mrow=mrow):
                for t in range(NT1):
                    e.wait_ge(SEM[1], bB + t + 1)
                    po = 2560 + 512 * (t % 2)
                    pinc(e.tensor_reduce(mx2[0:2, 4 + t:5 + t],
                                         ps_all[0:2, po:po + 512],
                                         axis=mybir.AxisListType.X, op=OP.max),
                         2)
                e.tensor_reduce(mx2[0:2, mrow:mrow + 1], mx2[0:2, 4:20],
                                axis=mybir.AxisListType.X, op=OP.max)
            group(("scalar", pp_sq), ("tensor", pp_mm), ("vector", pp_rd))

        sec("vector", lambda e: e.tensor_tensor(mx2[0:2, 2:3], mx2[0:2, 0:1],
                                                mx2[0:2, 1:2], OP.mult))
        sec("scalar", lambda e: e.activation(mx2[0:2, 2:3], mx2[0:2, 2:3],
                                             AF.Sqrt, bias=0.0, scale=1.0))
        sec("gpsimd", lambda e: e.tensor_reduce(mx2[0:1, 3:4], mx2[0:2, 2:3],
                                                axis=mybir.AxisListType.C,
                                                op=OP.max))
        sec("vector", lambda e: e.tensor_scalar(mx2[0:1, 3:4], mx2[0:1, 3:4],
                                                -0.125, None, op0=OP.mult))
        sec("tensor", lambda e: e.matmul(PS(2048, 2560)[:, 0:1], c_o1x128[:],
                                         mx2[0:1, 3:4], start=True, stop=True))
        sec("vector", lambda e, hp=hp: e.tensor_copy(negc[:, hp:hp + 1],
                                                     PS(2048, 2560)[:, 0:1]))

    if TRUNCV == 2:
        trunc_fin(q_bf[:, 0, 0:2048], 2048, stacks=(A2, A))
        return nc

    for hp in range(2):
        for qs in range(NT1):
            nkc = 4 * qs + 4
            bA = scnt[0]; bB = scnt[1]; bC = scnt[2]; bD = scnt[3]
            bE = scnt[4]; bF = scnt[5]; bG = scnt[6]

            def p2_t(e, hp=hp, qs=qs, nkc=nkc, bB=bB, bC=bC, bE=bE, bG=bG):
                for v in range(nkc + 1):
                    if v < nkc:
                        kc = v
                        dg = kc - 4 * qs
                        off = 128 * dg if dg >= 0 else 0
                        if kc >= 2:
                            e.wait_ge(SEM[1], bB + kc - 1)
                        sb0 = 1024 * (kc % 2)
                        for h in (0, 1):
                            ins = e.matmul(
                                ps_all[:, sb0 + 512 * h + off:
                                       sb0 + 512 * h + 512],
                                k_bf[64 * h:64 * h + 64, hp,
                                     128 * kc:128 * kc + 128],
                                q_bf[64 * h:64 * h + 64, hp,
                                     512 * qs + off:512 * qs + 512],
                                start=True, stop=True,
                                tile_position=(64 * h, 0))
                        pinc(ins, 0)
                    if v >= 1:
                        kc = v - 1
                        dg = kc - 4 * qs
                        off = 128 * dg if dg >= 0 else 0
                        if dg >= 0:
                            e.wait_ge(SEM[2], bC + dg + 1)
                        else:
                            e.wait_ge(SEM[1], bB + kc + 1)
                        for h in (0, 1):
                            ob = 2048 + 512 * h
                            ins = e.matmul(
                                ps_all[0:65, ob + off:ob + 512],
                                vt[2 * hp + h][:, kc, :],
                                pt_sb[:, kc % 2, 512 * h + off:512 * h + 512],
                                start=(kc == 0), stop=(kc == nkc - 1))
                        pinc(ins, 3)
                for h in (0, 1):
                    e.wait_ge(SEM[4], bE + h + 1)
                    if h == 1:
                        e.wait_ge(SEM[6], bG + 1)
                    src = lrec if h == 0 else lrec2
                    pinc(e.matmul(PS(3072, 3584)[0:64, 0:512], c_o1x64[:],
                                  src[0:1, 0:512], start=True, stop=True), 5)

            def p2_s(e, hp=hp, qs=qs, nkc=nkc, bA=bA, bD=bD):
                for kc in range(nkc):
                    dg = kc - 4 * qs
                    off = 128 * dg if dg >= 0 else 0
                    e.wait_ge(SEM[0], bA + kc + 1)
                    if kc >= 2:
                        e.wait_ge(SEM[3], bD + kc - 1)
                    sb0 = 1024 * (kc % 2)
                    src = ps_all[:, sb0:sb0 + 1024].rearrange(
                        "p (h q) -> p h q", h=2)[:, :, off:512]
                    dstv = pt_sb[:, kc % 2].rearrange(
                        "p (h q) -> p h q", h=2)[:, :, off:512]
                    pinc(e.activation(dstv, src, AF.Exp,
                                      bias=negc[:, hp:hp + 1], scale=0.125), 1)

            def p2_m(e, qs=qs, nkc=nkc, bB=bB):
                for kc in range(4 * qs, nkc):
                    dg = kc - 4 * qs
                    e.wait_ge(SEM[1], bB + kc + 1)
                    off = 128 * dg
                    for h in (0, 1):
                        ins = e.tensor_tensor(
                            pt_sb[:, kc % 2,
                                  512 * h + off:512 * h + off + 128],
                            pt_sb[:, kc % 2,
                                  512 * h + off:512 * h + off + 128],
                            c_trilo[:], OP.mult)
                    pinc(ins, 2)

            def p2_v(e, hp=hp, qs=qs, nkc=nkc, bD=bD, bF=bF):
                e.wait_ge(SEM[3], bD + nkc)
                for h in (0, 1):
                    ob = 2048 + 512 * h
                    dstr = lrec if h == 0 else lrec2
                    e.tensor_scalar(lrow[:], ps_all[64:65, ob:ob + 512], 1e-30,
                                    None, op0=OP.add)
                    pinc(e.reciprocal(dstr[:], lrow[:]), 4)
                for h in (0, 1):
                    e.wait_ge(SEM[5], bF + h + 1)
                    e.tensor_copy(obc[:], PS(3072, 3584)[0:64, 0:512])
                    pinc(e.tensor_tensor(
                        o_norm[64 * h:64 * h + 64, hp,
                               512 * qs:512 * qs + 512],
                        ps_all[0:64, 2048 + 512 * h:2560 + 512 * h],
                        obc[:], OP.mult), 6)
            group(("tensor", p2_t), ("scalar", p2_s), ("gpsimd", p2_m),
                  ("vector", p2_v))

    if DEBUG:
        def pd2(e):
            dma(e, "dsy", dbg["onorm"], o_norm[:])
            dwait(e, "dsy")
        sec("sync", pd2)

    if TRUNCV == 3:
        trunc_fin(o_norm[:, 0, 0:2048], 2048, stacks=(A2, A))
        return nc

    for tt in range(NT1):
        bA = scnt[0]; bH = scnt[7]

        def p3_t(e, tt=tt, bH=bH):
            for fc in range(8):
                a = 512 * (fc % 4)
                if fc >= 4:
                    e.wait_ge(SEM[7], bH + fc - 3)
                for ec in range(2):
                    ins = e.matmul(PS(0, 2048)[:, a:a + 512],
                                   woT_bf[:, ec, 128 * fc:128 * fc + 128],
                                   o_norm[:, ec, 512 * tt:512 * tt + 512],
                                   start=(ec == 0), stop=(ec == 1))
                pinc(ins, 0)

        def p3_v(e, bA=bA):
            for fc in range(8):
                a = 512 * (fc % 4)
                e.wait_ge(SEM[0], bA + fc + 1)
                pinc(e.tensor_copy(evp[:, fc, :], PS(0, 2048)[:, a:a + 512]), 7)

        def p3_st(e, tt=tt, bH=bH):
            r = tt // 4
            q4 = tt % 4
            e.wait_ge(SEM[7], bH + 8)
            dma(e, "dsy",
                cc_p_in[1024 * r:1024 * r + 1024,
                        512 * q4:512 * q4 + 512].rearrange(
                            "(c p) t -> p c t", p=128),
                evp[:])
            dwait(e, "dsy")
        group(("tensor", p3_t), ("vector", p3_v), ("sync", p3_st))

    def p3_cc(e):
        e.collective_compute(
            "ReduceScatter", OP.add,
            replica_groups=[[0, 1, 2, 3], [4, 5, 6, 7]],
            ins=[cc_p_in.ap().opt()], outs=[cc_p_out.ap().opt()],
        ).then_inc(ccp)
    sec("gpsimd", p3_cc)

    if TRUNCV == 4:
        trunc_fin(o_norm[:, 0, 0:2048], 2048, stacks=(A2, A),
                  wait_ccp=True)
        return nc

    A2.close()
    A.close()

    B = ExitStack()
    EB = B.enter_context
    big_bf = EB(nc.sbuf_tensor("big_bf", [128, 22528], BF16))
    xqin = big_bf[:, 0:8 * (LQ + 8)].rearrange("p (c t) -> p c t", c=8)
    o_loc = EB(nc.sbuf_tensor("o_loc", [128, 4, 2176], BF16))
    ptl = EB(nc.sbuf_tensor("ptl", [128, 2, 640], BF16))

    B1 = ExitStack()
    xlt = B1.enter_context(nc.sbuf_tensor("xlt2", [128, 2, 8, 512], F32))
    xcb = B1.enter_context(nc.sbuf_tensor("xcb2", [128, 2, 8, 512], BF16))
    xsqb = B1.enter_context(nc.sbuf_tensor("xsqb", [128, 8, 512], BF16))
    xnst = B1.enter_context(nc.sbuf_tensor("xnst", [128, 8, 512], F32))

    for t in range(6):
        bu = t % 2

        def b1_ld(e, t=t, bu=bu):
            dma(e, "dsy", xlt[:, bu],
                x_loc[:, 512 * t:512 * t + 512].rearrange(
                    "(c p) t -> p c t", p=128))
            dwait(e, "dsy")
        sec("sync", b1_ld)

        def b1_cast(e, t=t, bu=bu):
            dma(e, "dgp", xcb[:, bu],
                x_loc[:, 512 * t:512 * t + 512].rearrange(
                    "(c p) t -> p c t", p=128))
            dwait(e, "dgp")
        sec("gpsimd", b1_cast)

        def b1_sq(e, bu=bu):
            return e.activation(xsqb[:].rearrange("p c t -> p (c t)"),
                                xcb[:, bu].rearrange("p c t -> p (c t)"),
                                AF.Square, bias=0.0, scale=1.0)
        sec("scalar", b1_sq)

        def b1_mm(e, bu=bu):
            for ec in range(8):
                e.matmul(PS(1024, 1536)[0:1, 0:512], c_ones128[:],
                         xcb[:, bu, ec], start=(ec == 0), stop=(ec == 7))
            for ec in range(8):
                e.matmul(PS(1536, 2048)[0:1, 0:512], c_ones128[:],
                         xsqb[:, ec], start=(ec == 0), stop=(ec == 7))
        sec("tensor", b1_mm)

        def b1_st(e):
            e.tensor_scalar(stat1[:], PS(1024, 1536)[0:1, 0:512], 1.0 / D_,
                            None, op0=OP.mult)
            e.tensor_tensor(stat2[:], stat1[:], stat1[:], OP.mult)
            e.tensor_scalar(stat2[:], stat2[:], -1e-5, None, op0=OP.add)
            e.scalar_tensor_tensor(stat3[:], PS(1536, 2048)[0:1, 0:512],
                                   1.0 / D_, stat2[:], op0=OP.mult,
                                   op1=OP.subtract)
        sec("vector", b1_st)
        sec("scalar", lambda e: e.activation(stat3[:], stat3[:], AF.Sqrt,
                                             bias=0.0, scale=1.0))

        def b1_inv(e):
            e.reciprocal(stat2[:], stat3[:])
            e.tensor_tensor(stat1[:], stat1[:], stat2[:], OP.mult)
            e.tensor_scalar(stat1[:], stat1[:], -1.0, None, op0=OP.mult)
        sec("vector", b1_inv)

        def b1_bc(e):
            e.matmul(PS(2048, 2560)[:, 0:512], c_o1x128[:], stat2[:],
                     start=True, stop=True)
            e.matmul(PS(2560, 3072)[:, 0:512], c_o1x128[:], stat1[:],
                     start=True, stop=True)
        sec("tensor", b1_bc)

        def b1_bcc(e):
            e.tensor_copy(invb[:], PS(2048, 2560)[:, 0:512])
            e.tensor_copy(invb2[:], PS(2560, 3072)[:, 0:512])
        sec("vector", b1_bcc)

        def b1_xn(e, t=t, bu=bu):
            c0 = 512 * t - 376
            for ec in range(8):
                e.tensor_tensor(xnst[:, ec, :], xlt[:, bu, ec], invb[:],
                                OP.mult)
                e.tensor_tensor(xnst[:, ec, :], xnst[:, ec, :], invb2[:],
                                OP.add)
                lo = max(0, -c0)
                if c0 + 512 > 0:
                    e.tensor_copy(xqin[:, ec, c0 + lo:c0 + 512],
                                  xnst[:, ec, lo:512])
                if t == 1:
                    e.tensor_copy(xn_extra[:, ec, :], xnst[:, ec, 384:512])
        sec("vector", b1_xn)

        def b1_dma(e, t=t):
            d0 = 512 * t - 1023
            lo = max(0, -d0)
            if d0 + 512 > 0:
                dofn = dma_nc if 512 - lo == 1 else dma
                dofn(e, "dsy", xn_dram[:, :, d0 + lo:d0 + 512],
                     xnst[:, :, lo:512])
                dwait(e, "dsy")
        sec("sync", b1_dma)

    B1.close()
    if TRUNCV == 5:
        trunc_fin(big_bf[:, 0:2048], 2048, stacks=(B,))
        return nc

    if DEBUG:
        def pdxn(e):
            dma(e, "dsy", dbg["xn"], xqin)
            dwait(e, "dsy")
        sec("sync", pdxn)

    for l in range(2):
        LA = ExitStack()
        EL = LA.enter_context
        wqkvf_bf = EL(nc.sbuf_tensor(f"wqkvf_bf{l}", [128, 8, 1536], BF16))
        qfc = EL(nc.sbuf_tensor(f"qfc{l}", [128, 4, LQ], BF16))
        kfc = EL(nc.sbuf_tensor(f"kfc{l}", [128, 4, LQ], BF16))
        vtl = [EL(nc.sbuf_tensor(f"vtl{l}_{h}", [128, NKC, 65], BF16))
               for h in range(8)]
        xpt = EL(nc.sbuf_tensor(f"xpt{l}", [128, 4, 512], F32))

        def l_wload(e, l=l):
            dma(e, "dgp", wqkvf_bf[:],
                wqkvf[l].rearrange("(c p) f -> p c f", p=128))
            dwait(e, "dgp")
        sec("gpsimd", l_wload)

        def l_vones(e):
            for h in range(8):
                e.memset(vtl[h][:, :, 64:65], 1.0)
        sec("vector", l_vones)

        for (qo, qw) in QT:
            def l_tbl(e, qo=qo, qw=qw):
                for i, tbl in enumerate((xpAq, xpBq, xpAk, xpBk)):
                    dma(e, "dsy", xpt[:, i, 0:qw], tbl[:, 8 + qo:8 + qo + qw])
                dwait(e, "dsy")
            sec("sync", l_tbl)

            for fc in range(12):
                a = 512 * (fc % 2)

                if fc < 8:
                    isq = fc < 4
                    ch = fc % 4
                    dstf = qfc if isq else kfc
                    tA, tB = (0, 1) if isq else (2, 3)
                    bT = scnt[0]; bS = scnt[1]; bV = scnt[2]

                    def lq_t(e, fc=fc, a=a, qo=qo, qw=qw, bS=bS, bV=bV,
                             l=l, isq=isq):
                        for ec in range(8):
                            co = 8 + qo if ec < 4 else 7 + qo
                            ins = e.matmul(PS(0, 1024)[:, a:a + qw],
                                           wqkvf_bf[:, ec,
                                                    128 * fc:128 * fc + 128],
                                           xqin[:, ec, co:co + qw],
                                           start=(ec == 0), stop=(ec == 7))
                        pinc(ins, 0)
                        e.wait_ge(SEM[1], bS + 1)
                        e.matmul(ps_all[0:1, 2048:2048 + qw],
                                 c_indh2[:, 0:1], sqst[:, 0:qw],
                                 start=True, stop=True)
                        pinc(e.matmul(ps_all[0:1, 3072:3072 + qw],
                                      c_indh2[:, 1:2], sqst[:, 0:qw],
                                      start=True, stop=True), 0)
                        e.wait_ge(SEM[2], bV + 2)
                        scr = (c_qs if isq else c_ks)[0:1, 64 * l:64 * l + 64]
                        e.matmul(PS(2560, 3072)[0:64, 0:qw], scr,
                                 stat1[0:1, 0:qw], start=True, stop=True)
                        pinc(e.matmul(ps_all[64:128, 2560:2560 + qw],
                                      scr, stat2[0:1, 0:qw],
                                      start=True, stop=True), 0)

                    def lq_s(e, a=a, qw=qw, bT=bT, bV=bV):
                        e.wait_ge(SEM[0], bT + 1)
                        pinc(e.activation(sqst[:, 0:qw],
                                          PS(0, 1024)[:, a:a + qw],
                                          AF.Square, bias=0.0, scale=1.0), 1)
                        e.wait_ge(SEM[2], bV + 1)
                        e.activation(stat1[0:1, 0:qw], stat1[0:1, 0:qw],
                                     AF.Sqrt, bias=0.0, scale=1.0)
                        pinc(e.activation(stat2[0:1, 0:qw], stat2[0:1, 0:qw],
                                          AF.Sqrt, bias=0.0, scale=1.0), 1)

                    def lq_v(e, fc=fc, a=a, qo=qo, qw=qw, ch=ch, dstf=dstf,
                             tA=tA, tB=tB, bT=bT, bS=bS):
                        e.wait_ge(SEM[0], bT + 2)
                        e.tensor_scalar(stat1[0:1, 0:qw],
                                        ps_all[0:1, 2048:2048 + qw],
                                        1e-24, None, op0=OP.max)
                        pinc(e.tensor_scalar(stat2[0:1, 0:qw],
                                             ps_all[0:1, 3072:3072 + qw],
                                             1e-24, None, op0=OP.max), 2)
                        e.wait_ge(SEM[1], bS + 2)
                        e.reciprocal(stat1[0:1, 0:qw], stat1[0:1, 0:qw])
                        pinc(e.reciprocal(stat2[0:1, 0:qw], stat2[0:1, 0:qw]),
                             2)
                        e.wait_ge(SEM[0], bT + 3)
                        e.tensor_copy(invb[:, 0:qw],
                                      ps_all[0:128, 2560:2560 + qw])
                        e.tensor_tensor(ev32[:, 0:qw],
                                        PS(0, 1024)[:, a:a + qw],
                                        invb[:, 0:qw], OP.mult)
                        e.stream_shuffle(rope1[:, 0:qw], ev32[:, 0:qw],
                                         mask=SWAP32)
                        e.tensor_tensor(rope2[:, 0:qw], ev32[:, 0:qw],
                                        xpt[:, tA, 0:qw], OP.mult)
                        e.tensor_tensor(rope1[:, 0:qw], rope1[:, 0:qw],
                                        xpt[:, tB, 0:qw], OP.mult)
                        e.tensor_tensor(dstf[:, ch, qo:qo + qw],
                                        rope2[:, 0:qw], rope1[:, 0:qw],
                                        OP.add)
                    group(("tensor", lq_t), ("scalar", lq_s), ("vector", lq_v))
                else:
                    ch = fc - 8

                    def lv_t(e, fc=fc, a=a, qo=qo, qw=qw):
                        for ec in range(8):
                            co = 8 + qo if ec < 4 else 7 + qo
                            e.matmul(PS(0, 1024)[:, a:a + qw],
                                     wqkvf_bf[:, ec, 128 * fc:128 * fc + 128],
                                     xqin[:, ec, co:co + qw],
                                     start=(ec == 0), stop=(ec == 7))
                    sec("tensor", lv_t)

                    def lv_ev(e, a=a, qw=qw):
                        e.tensor_copy(vstage[:, 0:qw], PS(0, 1024)[:, a:a + qw])
                    sec("vector", lv_ev)

                    def lv_tr(e, qw=qw):
                        for hb in (0, 1):
                            for j in range(qw // 128):
                                sl = 64 * (2 * j + hb)
                                b0 = 2048 if hb == 0 else 3072
                                e.matmul(ps_all[:, b0 + 64 * j:
                                                b0 + 64 * j + 64],
                                         vstage[64 * hb:64 * hb + 64,
                                                128 * j:128 * j + 128],
                                         c_ident[64 * hb:64 * hb + 64, :],
                                         start=True, stop=True)
                    sec("tensor", lv_tr)

                    def lv_cp(e, ch=ch, qo=qo, qw=qw):
                        for hb in (0, 1):
                            for j in range(qw // 128):
                                b0 = 2048 if hb == 0 else 3072
                                e.tensor_copy(
                                    vtl[2 * ch + hb][:, qo // 128 + j, 0:64],
                                    ps_all[:, b0 + 64 * j:b0 + 64 * j + 64])
                    sec("vector", lv_cp)

        if DEBUG and l == 0:
            def pdqk(e):
                dma(e, "dsy", dbg["qh"], qfc[:])
                dma(e, "dsy", dbg["kh"], kfc[:])
                dwait(e, "dsy")
            sec("sync", pdqk)

        for h in range(8):
            ch, hb = h // 2, h % 2

            if hb == 0:
                bS0 = scnt[0]; bS1 = scnt[1]

                def nb_s(e, ch=ch, bS1=bS1):
                    for ci in range(38):
                        if ci >= 2:
                            e.wait_ge(SEM[1], bS1 + ci - 1)
                        par = sqst if ci % 2 == 0 else vstage
                        if ci < 17:
                            src = qfc[:, ch, 512 + 128 * ci:640 + 128 * ci]
                        else:
                            kc = ci - 17
                            src = kfc[:, ch, 128 * kc:128 * kc + 128]
                        pinc(e.activation(par[:, 0:128], src, AF.Abs,
                                          bias=0.0, scale=1.0), 0)

                def nb_v(e, bS0=bS0):
                    for ci in range(38):
                        e.wait_ge(SEM[0], bS0 + ci + 1)
                        par = sqst if ci % 2 == 0 else vstage
                        pinc(e.tensor_reduce(invb[:, ci:ci + 1], par[:, 0:128],
                                             axis=mybir.AxisListType.X,
                                             op=OP.max), 1)
                    for i in range(17):
                        e.tensor_reduce(invb[:, 40 + i:41 + i],
                                        invb[:, 17 + i:22 + i],
                                        axis=mybir.AxisListType.X, op=OP.max)
                    e.tensor_tensor(ev32[:, 0:17], invb[:, 0:17],
                                    invb[:, 40:57], OP.mult)
                    e.tensor_copy(sqst[:, 0:17], ev32[:, 0:17])
                group(("scalar", nb_s), ("vector", nb_v))

                def nb_t(e):
                    e.matmul(ps_all[0:1, 2048:2065], c_indh2[:, 0:1],
                             sqst[:, 0:17], start=True, stop=True)
                    e.matmul(ps_all[0:1, 3072:3089], c_indh2[:, 1:2],
                             sqst[:, 0:17], start=True, stop=True)
                sec("tensor", nb_t)

                def nb_v2(e):
                    e.tensor_scalar(stat1[0:1, 0:17], ps_all[0:1, 2048:2065],
                                    -8.24, 24.0, op0=OP.mult, op1=OP.add)
                    e.tensor_scalar(stat2[0:1, 0:17], ps_all[0:1, 3072:3089],
                                    -8.24, 24.0, op0=OP.mult, op1=OP.add)
                sec("vector", nb_v2)

                def nb_t2(e):
                    e.matmul(PS(2560, 3072)[:, 0:17], c_o1x128[:],
                             stat1[0:1, 0:17], start=True, stop=True)
                    e.matmul(ps_all[:, 3328:3345], c_o1x128[:],
                             stat2[0:1, 0:17], start=True, stop=True)
                sec("tensor", nb_t2)

                def nb_v3(e):
                    e.tensor_copy(ngcall[:, 0:17], PS(2560, 3072)[:, 0:17])
                    e.tensor_copy(invb[:, 100:117], ps_all[:, 3328:3345])
                sec("vector", nb_v3)

            bA = scnt[0]; bB = scnt[1]; bC = scnt[2]; bD = scnt[3]
            bE = scnt[4]; bF = scnt[5]
            bG6 = scnt[6]

            def la_t(e, h=h, ch=ch, hb=hb, bB=bB, bC=bC, bE=bE, bF=bF):
                for v in range(19):
                    if v < 17:
                        i = v
                        qc = 512 + 128 * i
                        if i >= 2:
                            e.wait_ge(SEM[1], bB + i - 1)
                        sb0 = 1024 * (i % 2)
                        for k5 in range(5):
                            ins = e.matmul(
                                ps_all[:, sb0 + 128 * k5:sb0 + 128 * k5 + 128],
                                kfc[64 * hb:64 * hb + 64, ch,
                                    128 * (i + k5):128 * (i + k5) + 128],
                                qfc[64 * hb:64 * hb + 64, ch, qc:qc + 128],
                                start=True, stop=True,
                                tile_position=(64 * hb, 0))
                        pinc(ins, 0)
                    if v >= 2:
                        i = v - 2
                        e.wait_ge(SEM[4], bE + i + 1)
                        pinc(e.matmul(ps_all[0:64, 2560:2688], c_o1x64[:],
                                      lrec[0:1, 0:128],
                                      start=True, stop=True), 6)
                    if 1 <= v <= 17:
                        i = v - 1
                        e.wait_ge(SEM[2], bC + i + 1)
                        if i >= 1:
                            e.wait_ge(SEM[5], bF + i)
                        for k5 in range(5):
                            ins = e.matmul(
                                ps_all[0:65, 2048:2048 + 128],
                                vtl[h][:, i + k5, :],
                                ptl[:, i % 2, 128 * k5:128 * k5 + 128],
                                start=(k5 == 0), stop=(k5 == 4))
                        pinc(ins, 3)

            def la_s(e, h=h, hb=hb, bA=bA, bD=bD):
                for i in range(17):
                    e.wait_ge(SEM[0], bA + i + 1)
                    if i >= 2:
                        e.wait_ge(SEM[3], bD + i - 1)
                    sb0 = 1024 * (i % 2)
                    bsrc = ngcall[:, i:i + 1] if hb == 0 \
                        else invb[:, 100 + i:101 + i]
                    pinc(e.activation(ptl[:, i % 2, 0:640],
                                      ps_all[:, sb0:sb0 + 640], AF.Exp,
                                      bias=bsrc, scale=8.0), 1)

            def la_m(e, bB=bB):
                for i in range(17):
                    e.wait_ge(SEM[1], bB + i + 1)
                    par = i % 2
                    e.tensor_tensor(ptl[:, par, 512:640], ptl[:, par, 512:640],
                                    c_trilo[:], OP.mult)
                    ins = e.tensor_tensor(ptl[:, par, 0:128],
                                          ptl[:, par, 0:128],
                                          c_trihi[:], OP.mult)
                    if 1 <= i <= 4:
                        for k5 in range(0, 5 - i):
                            ins = e.tensor_tensor(
                                ptl[:, par, 128 * k5:128 * k5 + 128],
                                ptl[:, par, 128 * k5:128 * k5 + 128],
                                c_fw01[:], OP.mult)
                    pinc(ins, 2)

            def la_v(e, h=h, ch=ch, hb=hb, bD=bD, bG6=bG6):
                for i in range(17):
                    e.wait_ge(SEM[3], bD + i + 1)
                    e.tensor_scalar(lrow[0:1, 0:128],
                                    ps_all[64:65, 2048:2176], 1e-30, None,
                                    op0=OP.add)
                    pinc(e.reciprocal(lrec[0:1, 0:128], lrow[0:1, 0:128]), 4)
                    e.wait_ge(SEM[6], bG6 + i + 1)
                    e.tensor_copy(obc[:, 0:128], ps_all[0:64, 2560:2688])
                    pinc(e.tensor_tensor(
                        o_loc[64 * hb:64 * hb + 64, ch,
                              128 * i:128 * i + 128],
                        ps_all[0:64, 2048:2176], obc[:, 0:128], OP.mult), 5)
            group(("tensor", la_t), ("scalar", la_s), ("gpsimd", la_m),
                  ("vector", la_v))

        if DEBUG and l == 0:
            def pdol(e):
                dma(e, "dsy", dbg["oloc"], o_loc[:])
                dwait(e, "dsy")
            sec("sync", pdol)

        LA.close()
        if TRUNCV == 6 and l == 0:
            trunc_fin(o_loc[:, 0, 0:2048], 2048, stacks=(B,))
            return nc

        LB = ExitStack()
        ELB = LB.enter_context
        wof_bf2 = ELB(nc.sbuf_tensor(f"wof2_{l}", [128, 4, 1024], BF16))
        xw8 = ELB(nc.sbuf_tensor(f"xw8_{l}", [128, 8, 513], F32))
        evp_b = ELB(nc.sbuf_tensor(f"evp_b{l}", [128, 8, 512], F32))

        src_dram = xn_dram if l == 0 else xcA
        tgt_dram = xcA if l == 0 else xcB

        def lb_w(e, l=l):
            dma(e, "dgp", wof_bf2[:], wof[l].rearrange("(c p) f -> p c f",
                                                       p=128))
            dwait(e, "dgp")
        sec("gpsimd", lb_w)

        for k4 in range(4):
            bA = scnt[0]; bH = scnt[7]

            def wo_ld(e, k4=k4, src_dram=src_dram):
                dma(e, "dsy", evp_b[:], src_dram[:, :, 1 + 512 * k4:
                                                 513 + 512 * k4])
                dwait(e, "dsy")

            def wo_t(e, k4=k4, bH=bH):
                for fc in range(8):
                    a = 512 * (fc % 4)
                    if fc >= 4:
                        e.wait_ge(SEM[7], bH + fc - 3)
                    for ec in range(4):
                        ins = e.matmul(PS(0, 2048)[:, a:a + 512],
                                       wof_bf2[:, ec, 128 * fc:128 * fc + 128],
                                       o_loc[:, ec, 128 + 512 * k4:
                                             640 + 512 * k4],
                                       start=(ec == 0), stop=(ec == 3))
                    pinc(ins, 0)

            def wo_v(e, k4=k4, bA=bA):
                dwait(e, "dsy")
                for fc in range(8):
                    a = 512 * (fc % 4)
                    e.wait_ge(SEM[0], bA + fc + 1)
                    pinc(e.tensor_tensor(evp_b[:, fc, :],
                                         PS(0, 2048)[:, a:a + 512],
                                         evp_b[:, fc, :], OP.add), 7)

            def wo_st(e, k4=k4, bH=bH, tgt_dram=tgt_dram):
                e.wait_ge(SEM[7], bH + 8)
                dma(e, "dsy", tgt_dram[:, :, 1 + 512 * k4:513 + 512 * k4],
                    evp_b[:])
                dwait(e, "dsy")
            group(("sync", wo_ld), ("tensor", wo_t), ("vector", wo_v))
            sec("sync", wo_st)

        bA = scnt[0]
        bH = scnt[7]

        def woe_t(e, bH=bH):
            for fc in range(8):
                a = 512 * (fc % 4)
                if fc >= 4:
                    e.wait_ge(SEM[7], bH + fc - 3)
                for ec in range(4):
                    ins = e.matmul(PS(0, 2048)[:, a:a + 128],
                                   wof_bf2[:, ec, 128 * fc:128 * fc + 128],
                                   o_loc[:, ec, 0:128],
                                   start=(ec == 0), stop=(ec == 3))
                pinc(ins, 0)

        def woe_v(e, l=l, bA=bA):
            for fc in range(8):
                a = 512 * (fc % 4)
                e.wait_ge(SEM[0], bA + fc + 1)
                if l == 0:
                    ins = e.tensor_tensor(mid_x[:, fc, :],
                                          PS(0, 2048)[:, a:a + 128],
                                          xn_extra[:, fc, :], OP.add)
                else:
                    ins = e.tensor_tensor(mid_x[:, fc, :],
                                          PS(0, 2048)[:, a:a + 128],
                                          xqin[:, fc, 520:648], OP.add)
                pinc(ins, 7)
        group(("tensor", woe_t), ("vector", woe_v))

        def woe_st(e, tgt_dram=tgt_dram):
            dma_nc(e, "dsy", tgt_dram[:, :, 0:1], mid_x[:, :, 127:128])
            dwait(e, "dsy")
        sec("sync", woe_st)

        LC = ExitStack()
        hshT = LC.enter_context(nc.sbuf_tensor(f"hsh{l}", [128, 8, 513], BF16))
        hbf = LC.enter_context(nc.sbuf_tensor(f"hbf{l}", [128, 8, LO], BF16))

        for t2 in range(4):
            def f_cast(e, t2=t2, tgt_dram=tgt_dram):
                dma(e, "dgp", hshT[:], tgt_dram[:, :, 512 * t2:512 * t2 + 513])
                dwait(e, "dgp")
            sec("gpsimd", f_cast)

            def f_ldx(e, t2=t2, tgt_dram=tgt_dram):
                dma(e, "dsy", xw8[:], tgt_dram[:, :, 512 * t2:512 * t2 + 513])
                dwait(e, "dsy")
            sec("sync", f_ldx)

            bS = scnt[1]
            bT = scnt[0]

            def f_sq(e, bT=bT):
                for ec in range(8):
                    o = 1 if ec < 4 else 0
                    if ec >= 2:
                        e.wait_ge(SEM[0], bT + ec - 1)
                    par = sqst if ec % 2 == 0 else vstage
                    pinc(e.activation(par[:], hshT[:, ec, o:o + 512],
                                      AF.Square, bias=0.0, scale=1.0), 1)

            def f_mm(e, bS=bS):
                for ec in range(8):
                    o = 1 if ec < 4 else 0
                    e.matmul(PS(1024, 1536)[0:1, 0:512], c_ones128[:],
                             hshT[:, ec, o:o + 512], start=(ec == 0),
                             stop=(ec == 7))
                for ec in range(8):
                    e.wait_ge(SEM[1], bS + ec + 1)
                    par = sqst if ec % 2 == 0 else vstage
                    pinc(e.matmul(PS(1536, 2048)[0:1, 0:512], c_ones128[:],
                                  par[:], start=(ec == 0), stop=(ec == 7)), 0)
            group(("scalar", f_sq), ("tensor", f_mm))

            def f_st(e):
                e.tensor_scalar(stat1[:], PS(1024, 1536)[0:1, 0:512], 1.0 / D_,
                                None, op0=OP.mult)
                e.tensor_tensor(stat2[:], stat1[:], stat1[:], OP.mult)
                e.tensor_scalar(stat2[:], stat2[:], -1e-5, None, op0=OP.add)
                e.scalar_tensor_tensor(stat3[:], PS(1536, 2048)[0:1, 0:512],
                                       1.0 / D_, stat2[:], op0=OP.mult,
                                       op1=OP.subtract)
            sec("vector", f_st)
            sec("scalar", lambda e: e.activation(stat3[:], stat3[:], AF.Sqrt,
                                                 bias=0.0, scale=1.0))

            def f_inv(e):
                e.reciprocal(stat2[:], stat3[:])
                e.tensor_tensor(stat1[:], stat1[:], stat2[:], OP.mult)
                e.tensor_scalar(stat1[:], stat1[:], -1.0, None, op0=OP.mult)
            sec("vector", f_inv)

            def f_bc(e):
                e.matmul(PS(2048, 2560)[:, 0:512], c_o1x128[:], stat2[:],
                         start=True, stop=True)
                e.matmul(PS(2560, 3072)[:, 0:512], c_o1x128[:], stat1[:],
                         start=True, stop=True)
            sec("tensor", f_bc)

            def f_bcc(e):
                e.tensor_copy(invb[:], PS(2048, 2560)[:, 0:512])
                e.tensor_copy(invb2[:], PS(2560, 3072)[:, 0:512])
            sec("vector", f_bcc)

            bV = scnt[2]
            bW3 = scnt[3]

            def f_a(e, t2=t2, bW3=bW3):
                for ec in range(8):
                    o = 1 if ec < 4 else 0
                    if ec >= 2:
                        e.wait_ge(SEM[3], bW3 + ec - 1)
                    dstp = ev32 if ec % 2 == 0 else ev32b
                    e.tensor_tensor(dstp[:], xw8[:, ec, o:o + 512],
                                    invb[:], OP.mult)
                    pinc(e.tensor_tensor(dstp[:], dstp[:], invb2[:], OP.add),
                         2)

            def f_h(e, l=l, t2=t2, bV=bV):
                last = None
                for ec in range(8):
                    e.wait_ge(SEM[2], bV + ec + 1)
                    srcp = ev32 if ec % 2 == 0 else ev32b
                    last = e.tensor_scalar(
                        hbf[:, ec, 512 * t2:512 * t2 + 512], srcp[:],
                        c_lg[:, 8 * l + ec:8 * l + ec + 1],
                        c_lb[:, 8 * l + ec:8 * l + ec + 1],
                        op0=OP.mult, op1=OP.add)
                    pinc(last, 3)
                return None
            group(("vector", f_a), ("gpsimd", f_h))

        w1s = LC.enter_context(nc.sbuf_tensor(f"w1s{l}", [128, 2, 2, 8, 128],
                                              BF16))
        w2s = LC.enter_context(nc.sbuf_tensor(f"w2s{l}", [128, 2, 512], BF16))
        vgl = big_bf[:, 0:11264].rearrange("p (c t) -> p c t", c=22)

        for sub2 in range(2):
            for q2 in (0, 1):
                bT = scnt[0]; bS = scnt[1]; bV = scnt[7]
                dsnap = []

                def f4_g(e, l=l, bT=bT, dsnap=dsnap):
                    for fc in range(22):
                        if fc >= 2:
                            e.wait_ge(SEM[0], bT + fc - 1)
                        par = fc % 2
                        dma(e, "dgp", w1s[:, par, 0, :, 0:FCS[fc]],
                            w1a[l][:, FCO[fc]:FCO[fc] + FCS[fc]].rearrange(
                                "(c p) f -> p c f", p=128), ls=par)
                        dma(e, "dgp", w1s[:, par, 1, :, 0:FCS[fc]],
                            w1g[l][:, FCO[fc]:FCO[fc] + FCS[fc]].rearrange(
                                "(c p) f -> p c f", p=128), ls=2 + par)
                        dsnap.append((lcnt[par], lcnt[2 + par]))

                def f4_t(e, sub2=sub2, q2=q2, bV=bV, dsnap=dsnap):
                    t0 = 1024 * sub2 + 512 * q2
                    for fc in range(22):
                        par = fc % 2
                        e.wait_ge(LS[par], dsnap[fc][0])
                        e.wait_ge(LS[2 + par], dsnap[fc][1])
                        if fc >= 1:
                            e.wait_ge(SEM[7], bV + fc)
                        for ag in (0, 1):
                            bank = 1024 * ag
                            for ec in range(8):
                                ins = e.matmul(
                                    ps_all[0:FCS[fc], bank:bank + 512],
                                    w1s[:, par, ag, ec, 0:FCS[fc]],
                                    hbf[:, ec, t0:t0 + 512],
                                    start=(ec == 0), stop=(ec == 7))
                        pinc(ins, 0)

                def f4_s(e, l=l, bT=bT, bV=bV):
                    for fc in range(22):
                        e.wait_ge(SEM[0], bT + fc + 1)
                        if fc >= 2:
                            e.wait_ge(SEM[7], bV + fc - 1)
                        pinc(e.activation(
                            gel[0:FCS[fc], fc % 2, :],
                            ps_all[0:FCS[fc], 1024:1536], AF.Gelu,
                            bias=c_b1g[0:FCS[fc],
                                       22 * l + fc:22 * l + fc + 1],
                            scale=1.0), 1)

                def f4_v(e, l=l, bS=bS):
                    for fc in range(22):
                        e.wait_ge(SEM[1], bS + fc + 1)
                        e.tensor_scalar(ev32[0:FCS[fc], :],
                                        ps_all[0:FCS[fc], 0:512],
                                        c_b1a[0:FCS[fc],
                                              22 * l + fc:22 * l + fc + 1],
                                        None, op0=OP.add)
                        pinc(e.tensor_tensor(
                            vgl[0:FCS[fc], fc, :],
                            ev32[0:FCS[fc], :], gel[0:FCS[fc], fc % 2, :],
                            OP.mult), 7)
                group(("gpsimd", f4_g), ("tensor", f4_t), ("scalar", f4_s),
                      ("vector", f4_v))

                def w2_xld(e, sub2=sub2, q2=q2, tgt_dram=tgt_dram):
                    t0 = 1 + 1024 * sub2 + 512 * q2
                    dma(e, "dsy", xw8[:, :, 0:512],
                        tgt_dram[:, :, t0:t0 + 512])
                    dwait(e, "dsy")
                sec("sync", w2_xld)

                for grp in (0, 1):
                    bT = scnt[0]
                    dsnap2 = []

                    def w2_g(e, l=l, grp=grp, bT=bT, dsnap2=dsnap2):
                        for fc in range(22):
                            if fc >= 2:
                                e.wait_ge(SEM[0], bT + fc - 1)
                            dma(e, "dgp", w2s[0:FCS[fc], fc % 2, :],
                                w2f[l][FCO[fc]:FCO[fc] + FCS[fc],
                                       512 * grp:512 * grp + 512],
                                ls=4 + fc % 2)
                            dsnap2.append(lcnt[4 + fc % 2])

                    def w2_t(e, q2=q2, bT=bT, dsnap2=dsnap2):
                        for fc in range(22):
                            e.wait_ge(LS[4 + fc % 2], dsnap2[fc])
                            for j4 in range(4):
                                ins = e.matmul(
                                    ps_all[:, 512 * j4:512 * j4 + 512],
                                    w2s[0:FCS[fc], fc % 2,
                                        128 * j4:128 * j4 + 128],
                                    vgl[0:FCS[fc], fc, :],
                                    start=(fc == 0), stop=(fc == 21))
                            pinc(ins, 0)

                    def w2_v(e, l=l, grp=grp, bT=bT):
                        e.wait_ge(SEM[0], bT + 22)
                        for j4 in range(4):
                            f2c = 4 * grp + j4
                            e.scalar_tensor_tensor(
                                xw8[:, f2c, 0:512],
                                ps_all[:, 512 * j4:512 * j4 + 512],
                                c_b2[:, 8 * l + f2c:8 * l + f2c + 1],
                                xw8[:, f2c, 0:512],
                                op0=OP.add, op1=OP.add)
                    group(("gpsimd", w2_g), ("tensor", w2_t),
                          ("vector", w2_v))

                def w2_st(e, sub2=sub2, q2=q2, tgt_dram=tgt_dram):
                    t0 = 1 + 1024 * sub2 + 512 * q2
                    dma(e, "dsy", tgt_dram[:, :, t0:t0 + 512],
                        xw8[:, :, 0:512])
                    dwait(e, "dsy")
                sec("sync", w2_st)

        LC.close()

        if DEBUG and l == 0:
            def pdx1(e):
                dma(e, "dsy", dbg["x1"], xcA[:, :, :])
                dwait(e, "dsy")
            sec("sync", pdx1)

        if l == 0:
            LT = ExitStack()
            tail = LT.enter_context(nc.sbuf_tensor("tail", [128, 8, HALO],
                                                   BF16))
            slot = LT.enter_context(nc.sbuf_tensor("slot", [128, 8, HALO],
                                                   BF16))

            def ex_c(e):
                dma(e, "dgp", tail[:], xcA[:, :, 2049 - HALO:2049])
                dwait(e, "dgp")
            sec("gpsimd", ex_c)

            for r in range(4):
                def ex_m(e, r=r):
                    e.tensor_scalar(slot[:].rearrange("p c t -> p (c t)"),
                                    tail[:].rearrange("p c t -> p (c t)"),
                                    c_sel[:, r:r + 1], None, op0=OP.mult)
                sec("vector", ex_m)

                def ex_s(e, r=r):
                    dma(e, "dsy",
                        cc_f_in[1024 * r:1024 * r + 1024, :].rearrange(
                            "(c p) t -> p c t", p=128), slot[:])
                    dwait(e, "dsy")
                sec("sync", ex_s)

            def ex_cc(e):
                e.collective_compute(
                    "ReduceScatter", OP.add,
                    replica_groups=[[0, 1, 2, 3], [4, 5, 6, 7]],
                    ins=[cc_f_in.ap().opt()], outs=[cc_f_out.ap().opt()],
                ).then_inc(SEM[6])
                scnt[6] += 1
                e.wait_ge(SEM[6], scnt[6])
            sec("gpsimd", ex_cc)

            def ex_b(e):
                dma(e, "dgp", xqin[:, :, 0:HALO],
                    cc_f_out[:, :].rearrange("(c p) t -> p c t", p=128))
                dma(e, "dgp", xqin[:, :, HALO:HALO + LO],
                    xcA[:, :, 1:2049])
                dwait(e, "dgp")
            sec("gpsimd", ex_b)
            LT.close()

        LB.close()

    if TRUNCV == 7:
        trunc_fin(big_bf[:, 0:2048], 2048, stacks=(B,))
        return nc

    B6 = ExitStack()
    f6a = B6.enter_context(nc.sbuf_tensor("f6a", [128, 8, 512], F32))
    f6b = B6.enter_context(nc.sbuf_tensor("f6b", [128, 8, 512], F32))
    f6c = B6.enter_context(nc.sbuf_tensor("f6c", [128, 8, 512], F32))

    for k4 in range(4):
        def p6_ld(e, k4=k4):
            if k4 == 0:
                e.wait_ge(ccp, 1)
            dma(e, "dsy", f6a[:],
                x_loc[:, 1024 + 512 * k4:1536 + 512 * k4].rearrange(
                    "(c p) t -> p c t", p=128))
            dma(e, "dsy", f6b[:],
                cc_p_out[:, 512 * k4:512 * k4 + 512].rearrange(
                    "(c p) t -> p c t", p=128))
            dma(e, "dsy", f6c[:], xcB[:, :, 1 + 512 * k4:513 + 512 * k4])
            dwait(e, "dsy")
        sec("sync", p6_ld)

        def p6_v(e):
            e.tensor_tensor(f6a[:].rearrange("p c t -> p (c t)"),
                            f6a[:].rearrange("p c t -> p (c t)"),
                            f6b[:].rearrange("p c t -> p (c t)"), OP.add)
            e.tensor_tensor(f6a[:].rearrange("p c t -> p (c t)"),
                            f6a[:].rearrange("p c t -> p (c t)"),
                            f6c[:].rearrange("p c t -> p (c t)"), OP.add)
        sec("vector", p6_v)

        def p6_st(e, k4=k4):
            dma(e, "dsy",
                y_out[:, 512 * k4:512 * k4 + 512].rearrange(
                    "(c p) t -> p c t", p=128), f6a[:])
            dwait(e, "dsy")
        sec("sync", p6_st)

    if DEBUG:
        def pdpr(e):
            dma(e, "dsy", dbg["proj"], cc_p_out[:, :])
            dwait(e, "dsy")
        sec("sync", pdpr)

    B6.close()
    B.close()
    outer.close()

    def _cov(clock, keys, tgt, need):
        if clock.get(tgt, -1) >= need:
            return True
        for key in keys:
            c = pincby.get(key)
            if c is None and _P1PB is not None and key[0] is not None:
                c = _P1PB.get(key)
            if c is not None and c.get(tgt, -1) >= need:
                return True
        return False

    real = {}
    for pC, pun, my_eng, my_idx, loc in defA:
        if not _cov(pC, pun, my_eng, my_idx):
            real[loc] = real.get(loc, 0) + 1
    for mC, mun, my_ei, my_eng, pqn, pcnt, pC, pun, loc in defB:
        if not (_cov(mC, mun, pqn, pcnt)
                or _cov(pC, pun, my_eng, my_ei + 1)):
            real[loc] = real.get(loc, 0) + 1
    import sys
    for loc, cnt in sorted(real.items()):
        print(f"AUTOSYNC REAL DMA HAZARD: {loc} x{cnt}", file=sys.stderr)
    if "barloc" in autostats:
        top = sorted(autostats["barloc"].items(), key=lambda kv: -kv[1][0])
        for (ln, fn2), c in top[:25]:
            print(f"BAR {fn2}@{ln} x{c[0]}", file=sys.stderr)
    globals()["_LAST_PINCBY"] = dict(pincby)
    print(f"AUTOSYNC: bars={autostats['bar']} realdma={sum(real.values())} "
          f"defA={len(defA)} defB={len(defB)} acnt={acnt} "
          f"chain={st['chain']} dsy={st['dsy']} dgp={st['dgp']} "
          f"scnt={scnt}", file=sys.stderr)
    return nc



def _host_tables():
    pos = np.arange(S_, dtype=np.float64)
    inv_freq = 1.0 / 10000.0 ** (np.arange(0, 32, 2, dtype=np.float64) / 32.0)
    sinus = pos[:, None] * inv_freq[None, :]
    sn, cs = np.sin(sinus), np.cos(sinus)
    C = np.zeros((128, S_), np.float32)
    Sg = np.zeros((128, S_), np.float32)
    C[32:64] = 1.0
    for i in range(16):
        C[2 * i] = cs[:, i]
        C[2 * i + 1] = cs[:, i]
        Sg[2 * i] = -sn[:, i]
        Sg[2 * i + 1] = sn[:, i]
    C[64:128] = C[0:64]
    Sg[64:128] = Sg[0:64]
    return C, Sg


def _xpos_tables(s0):
    t = np.arange(s0 - HALO, s0 + LO, dtype=np.float64)
    half = np.arange(0, 64, 2, dtype=np.float64)
    inv_f = 1.0 / 10000.0 ** (half / 64.0)
    fr = t[:, None] * inv_f[None, :]
    fr = np.concatenate([fr, fr], axis=1)
    base = (half + 0.4 * 64.0) / (1.4 * 64.0)
    base = np.concatenate([base, base])
    power = (t - S_ // 2) / 256.0
    scale = base[None, :] ** power[:, None]
    cs, sn = np.cos(fr), np.sin(fr)

    def mk(c_t, s_t):
        A64 = np.empty((64, c_t.shape[0]), np.float32)
        B64 = np.empty((64, c_t.shape[0]), np.float32)
        cT = c_t.T.astype(np.float32)
        sT = s_t.T.astype(np.float32)
        for j in range(32):
            A64[2 * j] = cT[j]
            A64[2 * j + 1] = cT[j + 32]
            B64[2 * j] = -sT[j]
            B64[2 * j + 1] = sT[j + 32]
        return (np.concatenate([A64, A64], 0),
                np.concatenate([B64, B64], 0))
    Aq, Bq = mk(cs * scale, sn * scale)
    Ak, Bk = mk(cs / scale, sn / scale)
    return Aq, Bq, Ak, Bk


PERM64 = np.empty(64, np.int64)
PERM64[0::2] = np.arange(32)
PERM64[1::2] = np.arange(32, 64)


def _permute_qk_cols(w):
    w = w.copy()
    for base in (0, 512):
        blk = w[:, base:base + 512].reshape(D_, 8, 64)
        w[:, base:base + 512] = blk[:, :, PERM64].reshape(D_, 512)
    return np.ascontiguousarray(w)


def kernel(L0_attn_hidden_states, L0_attn_w_qkv, L0_attn_w_out, L0_ffn_wqkv,
           L0_ffn_q_scale, L0_ffn_k_scale, L0_ffn_wo, L0_ffn_ln_g,
           L0_ffn_ln_b, L0_ffn_w1, L0_ffn_b1, L0_ffn_w2, L0_ffn_b2):
    x = np.asarray(L0_attn_hidden_states, np.float32)
    wqkv = np.asarray(L0_attn_w_qkv, np.float32)
    wout = np.asarray(L0_attn_w_out, np.float32)
    fwqkv = np.asarray(L0_ffn_wqkv, np.float32)
    fqs = np.asarray(L0_ffn_q_scale, np.float32)
    fks = np.asarray(L0_ffn_k_scale, np.float32)
    fwo = np.asarray(L0_ffn_wo, np.float32)
    flg = np.asarray(L0_ffn_ln_g, np.float32)
    flb = np.asarray(L0_ffn_ln_b, np.float32)
    fw1 = np.asarray(L0_ffn_w1, np.float32)
    fb1 = np.asarray(L0_ffn_b1, np.float32)
    fw2 = np.asarray(L0_ffn_w2, np.float32)
    fb2 = np.asarray(L0_ffn_b2, np.float32)

    C32, S32 = _host_tables()
    woT_full = np.ascontiguousarray(wout.T)

    tri_lo = np.tril(np.ones((128, 128), np.float32)).T.astype(BFD)
    tri_hi = np.tril(np.ones((128, 128), np.float32)).astype(BFD)

    in_maps = []
    for c in range(8):
        b, g = c // 4, c % 4
        s0 = 2048 * g
        xbT = np.ascontiguousarray(x[b].T)
        x_loc = np.zeros((D_, 3072), np.float32)
        lo_tok = s0 - 1024
        j0 = max(0, -lo_tok)
        x_loc[:, j0:] = xbT[:, lo_tok + j0:s0 + LO]
        wq = wqkv[768 * g:768 * g + 256]
        wv = wqkv[768 * g + 256:768 * g + 512]
        wk = wqkv[768 * g + 512:768 * g + 768]
        watt = np.ascontiguousarray(np.concatenate([wq, wk, wv], 0).T)
        woT = np.ascontiguousarray(woT_full[256 * g:256 * g + 256])
        Aq, Bq, Ak, Bk = _xpos_tables(s0)

        qs_c = np.zeros((1, 128), np.float32)
        ks_c = np.zeros((1, 128), np.float32)
        lg_c = np.zeros((128, 16), np.float32)
        lb_c = np.zeros((128, 16), np.float32)
        b1a_c = np.zeros((128, 44), np.float32)
        b1g_c = np.zeros((128, 44), np.float32)
        b2_c = np.zeros((128, 16), np.float32)
        for l in range(2):
            qs_c[0, 64 * l:64 * l + 64] = fqs[l][PERM64]
            ks_c[0, 64 * l:64 * l + 64] = fks[l][PERM64]
            for ec in range(8):
                lg_c[:, 8 * l + ec] = flg[l, 128 * ec:128 * ec + 128]
                lb_c[:, 8 * l + ec] = flb[l, 128 * ec:128 * ec + 128]
                b2_c[:, 8 * l + ec] = fb2[l, 128 * ec:128 * ec + 128]
            for fc in range(22):
                b1a_c[0:FCS[fc], 22 * l + fc] = \
                    fb1[l, FCO[fc]:FCO[fc] + FCS[fc]]
                b1g_c[0:FCS[fc], 22 * l + fc] = \
                    fb1[l, HID + FCO[fc]:HID + FCO[fc] + FCS[fc]]

        sel = np.zeros((128, 4), np.float32)
        if g < 3:
            sel[:, g + 1] = 1.0
        indh2 = np.zeros((128, 2), np.float32)
        indh2[0:64, 0] = 1.0
        indh2[64:128, 1] = 1.0

        m = dict(
            x_fm=xbT, x_loc=x_loc, watt=watt, woT=woT, ropc=C32, rops=S32,
            wqkvf0=_permute_qk_cols(fwqkv[0]),
            wqkvf1=_permute_qk_cols(fwqkv[1]),
            wof0=np.ascontiguousarray(fwo[0]),
            wof1=np.ascontiguousarray(fwo[1]),
            w1a0=np.ascontiguousarray(fw1[0][:, :HID]),
            w1a1=np.ascontiguousarray(fw1[1][:, :HID]),
            w1g0=np.ascontiguousarray(fw1[0][:, HID:]),
            w1g1=np.ascontiguousarray(fw1[1][:, HID:]),
            w2f0=np.ascontiguousarray(fw2[0]),
            w2f1=np.ascontiguousarray(fw2[1]),
            qs_col=qs_c, ks_col=ks_c, lg_col=lg_c, lb_col=lb_c,
            b1a_col=b1a_c, b1g_col=b1g_c, b2_col=b2_c,
            xpAq=Aq, xpBq=Bq, xpAk=Ak, xpBk=Bk,
            tri_lo=tri_lo, tri_hi=tri_hi,
            fw01=(np.ones if g > 0 else np.zeros)((128, 128)).astype(BFD),
            sel_cols=sel,
            ones128_bf=np.ones((128, 1), BFD),
            ones64_bf=np.ones((64, 1), BFD),
            ones1x128=np.ones((1, 128), np.float32),
            ones1x64=np.ones((1, 64), np.float32),
            indh2_bf=indh2.astype(BFD),
            ident_bf=np.vstack([np.eye(64), np.eye(64)]).astype(BFD),
            ident_f32=np.vstack([np.eye(64), np.eye(64)]).astype(np.float32),
        )
        in_maps.append(m)

    globals()["_PASS1_PINCBY"] = None
    build()
    globals()["_PASS1_PINCBY"] = globals().get("_LAST_PINCBY")
    nc = build()
    globals()["_PASS1_PINCBY"] = None
    res = run_bass_kernel_spmd(nc, in_maps, core_ids=list(range(8)))
    y = np.empty((B_, S_, D_), np.float32)
    for c in range(8):
        b, g = c // 4, c % 4
        y[b, 2048 * g:2048 * g + 2048, :] = \
            np.asarray(res.results[c]["y"], np.float32).T
    if DEBUG:
        kernel._dbg = [res.results[c] for c in range(8)]
    return y
